# version 17
# baseline (speedup 1.0000x reference)
import numpy as np
import concourse.bass as bass
import concourse.mybir as mybir
from concourse.bass_utils import run_bass_kernel_spmd
from contextlib import ExitStack

F32 = mybir.dt.float32
BF16 = mybir.dt.bfloat16
FP8 = mybir.dt.float8e4
AF = mybir.ActivationFunctionType
ALU = mybir.AluOpType
AXX = mybir.AxisListType.X

D = 2048
KD = 16
WIN = 512
NH = 16
HD = 128
II = 5632
KI = 44
NIC = 11
TOWN = 1024
TALL = 1536
NCO = 8
NCA = 12
CSC = float(1.0 / np.sqrt(HD))
NEG = -1e30


class Em:

    ENGS = ("sync", "gpsimd", "scalar", "tensor", "vector")

    def __init__(self):
        self.streams = {e: [] for e in self.ENGS}
        self.counts = {}
        self.watermark = {}
        self.last = {e: None for e in self.ENGS}

    def op(self, eng, fn, inc=None):
        rec = {"fn": fn, "incs": []}
        if inc is not None:
            self.add_inc(rec, inc[0], inc[1])
        self.streams[eng].append(("op", rec))
        self.last[eng] = rec
        return rec

    def add_inc(self, rec, sem, n):
        rec["incs"].append((sem, n))
        self.counts[sem] = self.counts.get(sem, 0) + n

    def inc_last(self, eng, sem, n=1):
        self.add_inc(self.last[eng], sem, n)

    def wait(self, eng, sem, cnt=None):
        if cnt is None:
            cnt = self.counts.get(sem, 0)
        if cnt <= 0:
            return
        key = (eng, sem)
        if self.watermark.get(key, -1) >= cnt:
            return
        self.watermark[key] = cnt
        self.streams[eng].append(("wait", sem, cnt))

    def bar(self, eng, sem):
        rec = self.last[eng]
        if not any(s == sem for s, _ in rec["incs"]):
            self.add_inc(rec, sem, 1)
        self.wait(eng, sem, self.counts[sem])

    def phase_barrier(self, exclude=()):
        for eng in self.ENGS:
            for sem in list(self.counts):
                if sem in exclude:
                    continue
                self.wait(eng, sem)

    def replay(self, block, sems):
        for eng in self.ENGS:
            stream = self.streams[eng]
            if not stream:
                continue

            def body(engine, stream=stream):
                for kind, *rest in stream:
                    if kind == "wait":
                        sem, cnt = rest
                        engine.wait_ge(sems[sem], cnt)
                    else:
                        rec = rest[0]
                        ins = rec["fn"](engine)
                        for sem, n in rec["incs"]:
                            ins.then_inc(sems[sem], n)
            getattr(block, eng)(body)


def build(debug=False, no_collective=False):
    nc = bass.Bass("TRN2", target_bir_lowering=False)

    xh = nc.dram_tensor("xh", [TALL, D], F32, kind="ExternalInput").ap()
    wq = nc.dram_tensor("wq", [D, D], F32, kind="ExternalInput").ap()
    wk = nc.dram_tensor("wk", [D, 512], F32, kind="ExternalInput").ap()
    wv = nc.dram_tensor("wv", [D, 512], F32, kind="ExternalInput").ap()
    wo = nc.dram_tensor("wo", [D, D], F32, kind="ExternalInput").ap()
    wgs = nc.dram_tensor("wgs", [6 * 128, D], F32, kind="ExternalInput").ap()
    wus = nc.dram_tensor("wus", [6 * 128, D], F32, kind="ExternalInput").ap()
    wds = nc.dram_tensor("wds", [2 * 128, II], F32, kind="ExternalInput").ap()
    m0 = nc.dram_tensor("m0", [4, 128, 512], F32, kind="ExternalInput").ap()
    yo = nc.dram_tensor("y", [TOWN, D], F32, kind="ExternalOutput").ap()
    tsc = nc.dram_tensor("tscr", [TOWN, II], F32, kind="Internal").ap()
    afm_d = nc.dram_tensor("afm_d", [128, 2 * NH * 512], BF16, kind="Internal").ap()
    NT_SH = 280
    cin = nc.dram_tensor("cin", [NT_SH * 128, 128], BF16, kind="Internal").ap()
    cout = nc.dram_tensor("cout", [8 * NT_SH * 128, 128], BF16, kind="Internal",
                          addr_space="Local" if no_collective else "Shared").ap()

    em = Em()
    SEMS = ["sLD", "sGP", "sGI", "sPE", "sDV", "sAC", "sOUT", "sCL"]

    def dma_sync(out, in_):
        em.wait("sync", "sLD")
        em.op("sync", lambda e, o=out, i=in_: e.dma_start(out=o, in_=i), inc=("sLD", 16))

    def dma_gp(out, in_):
        em.wait("gpsimd", "sGP")
        em.op("gpsimd", lambda e, o=out, i=in_: e.dma_start(out=o, in_=i), inc=("sGP", 16))

    def dma_out(out, in_):
        em.wait("gpsimd", "sOUT")
        em.op("gpsimd", lambda e, o=out, i=in_: e.dma_start(out=o, in_=i), inc=("sOUT", 16))

    def tap(name, tile_ap, shape, dtype):
        if not debug:
            return
        d = nc.dram_tensor("dbg_" + name, shape, dtype, kind="ExternalOutput").ap()
        for s in ("sDV", "sAC", "sPE", "sGI", "sLD"):
            em.wait("gpsimd", s)
        em.op("gpsimd", lambda e, o=d, i=tile_ap: e.dma_start(out=o, in_=i),
              inc=("sOUT", 16))

    def dv(fn, inc=True):
        em.op("vector", fn, inc=("sDV", 1) if inc else None)

    def ac(fn, inc=True):
        em.op("scalar", fn, inc=("sAC", 1) if inc else None)

    def pe(fn):
        em.op("tensor", fn)

    def gp(fn):
        em.op("gpsimd", fn, inc=("sGI", 1))

    with ExitStack() as root:
        def sb(scope, name, shape, dtype=F32, side=None):
            if side is None:
                return scope.enter_context(nc.sbuf_tensor(name, shape, dtype))
            return scope.enter_context(nc.sbuf_tensor(name, shape, dtype, side=side))

        def psm(scope, name, shape, dtype=F32):
            return scope.enter_context(nc.psum_tensor(name, shape, dtype))

        identb = sb(root, "identb", [128, 128], BF16)
        maskp = sb(root, "maskp", [128, 128], F32)
        maskc = sb(root, "maskc", [128, 128], F32)
        m0sb = sb(root, "m0sb", [128, 4 * 512], F32)
        qxT = sb(root, "qxT", [128, KD * TOWN], FP8)
        dxs = sb(root, "dxs", [128, NCO], F32)
        sts = sb(root, "sts", [128, NCO], F32)
        sm = {}
        for n in ("ssum", "ssq", "vv", "m2", "sd", "rstd", "nb",
                  "rmx", "ri", "sc8", "nm", "se", "rs", "wm", "dw", "tmp1", "eps"):
            sm[n] = sb(root, "sm_" + n, [128, 1], F32)

        gp(lambda e: e.memset(sm["eps"][:], 1e-5))
        gp(lambda e: e.memset(identb[:], 0.0))
        gp(lambda e: e.memset(maskp[:], 0.0))
        gp(lambda e: e.memset(maskc[:], 0.0))
        em.bar("gpsimd", "sGI")
        gp(lambda e: e.affine_select(out=identb[:], in_=identb[:],
                                     compare_op=ALU.not_equal, fill=1.0, base=0,
                                     pattern=[[-1, 128]], channel_multiplier=1))
        em.bar("gpsimd", "sGI")
        gp(lambda e: e.affine_select(out=maskp[:], in_=maskp[:],
                                     compare_op=ALU.is_gt, fill=NEG, base=0,
                                     pattern=[[1, 128]], channel_multiplier=-1))
        em.bar("gpsimd", "sGI")
        gp(lambda e: e.affine_select(out=maskc[:], in_=maskc[:],
                                     compare_op=ALU.is_ge, fill=NEG, base=0,
                                     pattern=[[-1, 128]], channel_multiplier=1))
        dma_gp(m0sb[:].rearrange("p (j c) -> p j c", j=4), m0[:, :, :].transpose([1, 0, 2]))

        with ExitStack() as sc_s:
            sld = [sb(sc_s, "sldA", [128, II], F32), sb(sc_s, "sldB", [128, II], F32)]
            s8 = [sb(sc_s, "s8A", [128, II], FP8), sb(sc_s, "s8B", [128, II], FP8)]
            sbf = sb(sc_s, "sbf", [128, II], BF16)
            stg_t = [sb(sc_s, "stgA", [128, KI * 128], BF16),
                     sb(sc_s, "stgB", [128, KI * 128], BF16)]
            smS = {}
            for n2 in ("wmA", "wmB", "riA", "riB", "scA", "scB", "dwA", "dwB"):
                smS[n2] = sb(sc_s, "smS_" + n2, [128, 1], F32)
            pS = [psm(sc_s, "pS0", [128, 1024], BF16), psm(sc_s, "pS1", [128, 1024], BF16)]
            st_s = {"ld": [0, 0], "pS": [0, 0], "rot": 0, "stg": [0, 0], "sbf": 0}

            def stage_chunk(idx, src_rows, width, nk, tilebase):
                par = idx % 2
                P = "AB"[par]
                em.wait("sync", "sAC", st_s["ld"][par])
                dma_sync(sld[par][:, 0:width], src_rows)
                em.wait("vector", "sLD")
                dv(lambda e, par=par, w=width: e.tensor_reduce(
                    out=smS["wm" + P][:], in_=sld[par][:, 0:w], axis=AXX,
                    op=ALU.max, apply_absolute_value=True))
                em.bar("vector", "sDV")
                dv(lambda e: e.reciprocal(out=smS["ri" + P][:], in_=smS["wm" + P][:]))
                em.bar("vector", "sDV")
                dv(lambda e: e.tensor_scalar(out=smS["sc" + P][:], in0=smS["ri" + P][:],
                                             scalar1=224.0, scalar2=None, op0=ALU.mult))
                dv(lambda e: e.tensor_scalar(out=smS["dw" + P][:], in0=smS["wm" + P][:],
                                             scalar1=1.0 / 224.0, scalar2=None,
                                             op0=ALU.mult))
                em.wait("scalar", "sDV")
                ac(lambda e, par=par, w=width: e.activation(
                    out=s8[par][:, 0:w], in_=sld[par][:, 0:w], func=AF.Copy,
                    scale=smS["sc" + P][:]))
                st_s["ld"][par] = em.counts["sAC"]
                em.wait("vector", "sAC")
                em.wait("vector", "sPE", st_s["sbf"])
                dv(lambda e, par=par, w=width: e.tensor_scalar(
                    out=sbf[:, 0:w], in0=s8[par][:, 0:w],
                    scalar1=smS["dw" + P][:], scalar2=None, op0=ALU.mult))
                em.wait("tensor", "sDV")
                em.wait("tensor", "sGI")
                em.wait("vector", "sGP", st_s["stg"][par])
                nbatch = (nk + 7) // 8
                for b in range(nbatch):
                    nb = min(8, nk - b * 8)
                    rot = st_s["rot"] % 2
                    st_s["rot"] += 1
                    em.wait("tensor", "sDV", st_s["pS"][rot])
                    for i in range(nb):
                        k = b * 8 + i
                        pe(lambda e, i=i, k=k, r=rot: e.transpose(
                            pS[r][:, i * 128:(i + 1) * 128],
                            sbf[:, k * 128:(k + 1) * 128], identb[:]))
                    em.inc_last("tensor", "sPE", 1)
                    em.wait("vector", "sPE")
                    dv(lambda e, b=b, nb=nb, r=rot, par=par: e.tensor_copy(
                        out=stg_t[par][:, b * 1024:b * 1024 + nb * 128],
                        in_=pS[r][:, 0:nb * 128]))
                    st_s["pS"][rot] = em.counts["sDV"]
                st_s["sbf"] = em.counts["sPE"]
                em.wait("gpsimd", "sDV")
                em.wait("gpsimd", "sGP")
                em.op("gpsimd", lambda e, par=par, nk=nk, tb=tilebase: e.dma_start(
                    out=cin[tb * 128:(tb + nk) * 128, :]
                    .rearrange("(k p) n -> p k n", p=128),
                    in_=stg_t[par][:, 0:nk * 128]
                    .rearrange("p (k n) -> p k n", k=nk)), inc=("sGP", 16))
                st_s["stg"][par] = em.counts["sGP"]

            idx = 0
            for i6 in range(6):
                stage_chunk(idx, wgs[i6 * 128:(i6 + 1) * 128, 0:D], D, KD, i6 * 16)
                idx += 1
            for i6 in range(6):
                stage_chunk(idx, wus[i6 * 128:(i6 + 1) * 128, 0:D], D, KD, 96 + i6 * 16)
                idx += 1
            for i2 in range(2):
                stage_chunk(idx, wds[i2 * 128:(i2 + 1) * 128, 0:II], II, KI,
                            192 + i2 * KI)
                idx += 1
            em.wait("gpsimd", "sGP")
            if no_collective:
                for cc in range(8):
                    em.wait("gpsimd", "sGP")
                    em.op("gpsimd", lambda e, cc=cc: e.dma_start(
                        out=cout[cc * NT_SH * 128:(cc + 1) * NT_SH * 128, :],
                        in_=cin[:, :]), inc=("sGP", 16))
                em.wait("gpsimd", "sGP")
                em.op("gpsimd", lambda e: e.engine_nop(), inc=("sCL", 1))
            else:
                em.op("gpsimd", lambda e: e.collective_compute(
                    "AllGather", ALU.bypass, ins=[cin], outs=[cout],
                    replica_groups=[list(range(8))]), inc=("sCL", 1))
            em.phase_barrier(exclude=("sCL",))

        rmt = sb(root, "rmt", [128, NCO], F32)
        sc_qkv = root.enter_context(ExitStack())
        qT = sb(sc_qkv, "qT", [128, NH * TOWN], BF16)
        kfm = sb(sc_qkv, "kfm", [128, 4 * TALL], BF16)
        vtm = sb(sc_qkv, "vtm", [128, NCA * 512], BF16)
        with ExitStack() as sc_ab:
            xnT = sb(sc_ab, "xnT", [128, KD * TALL], BF16)

            with ExitStack() as sc_a:
                xld = sb(sc_a, "xld", [128, 2 * D], F32)
                sqscr = sb(sc_a, "sqscr", [128, D], BF16)
                xn32 = [sb(sc_a, "xn32A", [128, D], F32),
                        sb(sc_a, "xn32B", [128, D], F32)]
                xnbf = [sb(sc_a, "xnbfA", [128, D], BF16),
                        sb(sc_a, "xnbfB", [128, D], BF16)]
                qx8c = [sb(sc_a, "qx8cA", [128, D], FP8),
                        sb(sc_a, "qx8cB", [128, D], FP8)]
                qxbf = [sb(sc_a, "qxbfA", [128, D], BF16),
                        sb(sc_a, "qxbfB", [128, D], BF16)]
                smA = {}
                for n2 in ("ssum", "ssq", "vv", "m2", "sd", "rstd", "nb", "rmx",
                           "ri", "sc8"):
                    smA[n2 + "A"] = sb(sc_a, "smA_" + n2 + "A", [128, 1], F32)
                    smA[n2 + "B"] = sb(sc_a, "smA_" + n2 + "B", [128, 1], F32)
                pT = [psm(sc_a, "pTa0", [128, 1024], BF16),
                      psm(sc_a, "pTa1", [128, 1024], BF16)]
                pT_cons = [0, 0]
                acons = {"xn32": [0, 0], "xnbf": [0, 0], "qx8c": [0, 0],
                         "qxbf": [0, 0], "xld": [0, 0], "xld_dv": [0, 0],
                         "ssq": [0, 0]}

                xnT3 = xnT[:].rearrange("p (k t) -> p k t", k=KD)
                qxT3 = qxT[:].rearrange("p (k t) -> p k t", k=KD)

                def transpose_chunk(src, dest3, col0, ncols=128):
                    em.wait("tensor", "sDV")
                    em.wait("tensor", "sGI")
                    for half in range(2):
                        pbuf = pT[half]
                        em.wait("tensor", "sDV", pT_cons[half])
                        for i in range(8):
                            k = half * 8 + i
                            pe(lambda e, pb=pbuf, i=i, k=k, s=src: e.transpose(
                                pb[:, i * 128:(i + 1) * 128],
                                s[:, k * 128:(k + 1) * 128], identb[:]))
                        em.inc_last("tensor", "sPE", 1)
                        em.wait("vector", "sPE")
                        dv(lambda e, pb=pbuf, half=half, d3=dest3, c0=col0: e.tensor_copy(
                            out=d3[:, half * 8:half * 8 + 8, c0:c0 + 128],
                            in_=pb[:].rearrange("p (k t) -> p k t", k=8)))
                        pT_cons[half] = em.counts["sDV"]

                for cn in range(NCA):
                    par = cn % 2
                    P = "AB"[par]
                    buf = xld[:, par * D:par * D + D]
                    em.wait("sync", "sAC", acons["xld"][par])
                    em.wait("sync", "sDV", acons["xld_dv"][par])
                    dma_sync(buf, xh[cn * 128:(cn + 1) * 128, :])
                    em.wait("vector", "sLD")
                    dv(lambda e, b=buf, P=P: e.reduce_sum(out=smA["ssum" + P][:], in_=b,
                                                          axis=AXX))
                    em.bar("vector", "sDV")
                    em.wait("scalar", "sLD")
                    em.wait("scalar", "sDV", acons["ssq"][par])
                    ac(lambda e, b=buf, P=P: e.activation(out=sqscr[:], in_=b,
                                                          func=AF.Square,
                                                          accum_out=smA["ssq" + P][:]))
                    dv(lambda e, P=P: e.scalar_tensor_tensor(
                        out=smA["m2" + P][:], in0=smA["ssum" + P][:],
                        scalar=1.0 / (D * float(D)), in1=smA["ssum" + P][:],
                        op0=ALU.mult, op1=ALU.mult))
                    em.bar("vector", "sDV")
                    em.wait("vector", "sAC")
                    dv(lambda e, P=P: e.scalar_tensor_tensor(
                        out=smA["vv" + P][:], in0=smA["ssq" + P][:],
                        scalar=1.0 / D, in1=smA["m2" + P][:],
                        op0=ALU.mult, op1=ALU.subtract))
                    em.bar("vector", "sDV")
                    acons["ssq"][par] = em.counts["sDV"]
                    em.wait("scalar", "sDV")
                    em.wait("scalar", "sGI")
                    ac(lambda e, P=P: e.activation(out=smA["sd" + P][:],
                                                   in_=smA["vv" + P][:], func=AF.Sqrt,
                                                   bias=sm["eps"][:]))
                    em.wait("vector", "sAC")
                    dv(lambda e, P=P: e.reciprocal(out=smA["rstd" + P][:],
                                                   in_=smA["sd" + P][:]))
                    em.bar("vector", "sDV")
                    dv(lambda e, P=P: e.scalar_tensor_tensor(
                        out=smA["nb" + P][:], in0=smA["ssum" + P][:],
                        scalar=-1.0 / D, in1=smA["rstd" + P][:],
                        op0=ALU.mult, op1=ALU.mult))
                    em.wait("scalar", "sDV")
                    em.wait("scalar", "sDV", acons["xn32"][par])
                    ac(lambda e, b=buf, P=P, par=par: e.activation(
                        out=xn32[par][:], in_=b, func=AF.Identity,
                        scale=smA["rstd" + P][:], bias=smA["nb" + P][:]))
                    acons["xld"][par] = em.counts["sAC"]
                    acons["xld_dv"][par] = em.counts["sDV"]
                    em.wait("vector", "sAC")
                    em.wait("vector", "sPE", acons["xnbf"][par])
                    dv(lambda e, par=par: e.tensor_copy(out=xnbf[par][:],
                                                        in_=xn32[par][:]))
                    transpose_chunk(xnbf[par], xnT3, cn * 128)
                    acons["xnbf"][par] = em.counts["sPE"]
                    if cn >= 4:
                        co = cn - 4
                        dv(lambda e, P=P, par=par: e.tensor_reduce(
                            out=smA["rmx" + P][:], in_=xn32[par][:], axis=AXX,
                            op=ALU.max, apply_absolute_value=True))
                        em.bar("vector", "sDV")
                        dv(lambda e, P=P: e.reciprocal(out=smA["ri" + P][:],
                                                       in_=smA["rmx" + P][:]))
                        em.bar("vector", "sDV")
                        dv(lambda e, P=P: e.tensor_scalar(
                            out=smA["sc8" + P][:], in0=smA["ri" + P][:],
                            scalar1=224.0, scalar2=None, op0=ALU.mult))
                        dv(lambda e, co=co, P=P: e.tensor_scalar(
                            out=dxs[:, co:co + 1], in0=smA["rmx" + P][:],
                            scalar1=1.0 / 224.0, scalar2=None, op0=ALU.mult))
                        em.wait("scalar", "sDV")
                        ac(lambda e, P=P, par=par: e.activation(
                            out=qx8c[par][:], in_=xn32[par][:], func=AF.Copy,
                            scale=smA["sc8" + P][:]))
                        acons["xn32"][par] = em.counts["sAC"]
                        em.wait("vector", "sAC")
                        em.wait("vector", "sPE", acons["qxbf"][par])
                        dv(lambda e, par=par: e.tensor_copy(out=qxbf[par][:],
                                                            in_=qx8c[par][:]))
                        transpose_chunk(qxbf[par], qxT3, co * 128)
                        acons["qxbf"][par] = em.counts["sPE"]
                    acons["xn32"][par] = em.counts["sDV"]

            tap("xnT", xnT[:], [128, KD * TALL], BF16)
            tap("qxT", qxT[:], [128, KD * TOWN], FP8)
            tap("dxs", dxs[:], [128, NCO], F32)
            em.phase_barrier()


            with ExitStack() as sc_bp:
                wbuf = sb(sc_bp, "wbuf", [128, KD * 1024], BF16)
                pj = [psm(sc_bp, "pj0", [128, 512], F32),
                      psm(sc_bp, "pj1", [128, 512], F32)]
                pj_cons = [0, 0]
                wbuf3_1024 = wbuf[:].rearrange("p (k n) -> p k n", k=KD)
                wbuf3_512 = wbuf[:].rearrange("p (k n) -> p k n", k=2 * KD)

                def proj_group(lhs_col_fn, rhs_col_fn, dest, pslot, scale=None):
                    em.wait("tensor", "sDV", pj_cons[pslot])
                    for k in range(KD):
                        lc, rc = lhs_col_fn(k), rhs_col_fn(k)
                        pe(lambda e, lc=lc, rc=rc, k=k, p=pj[pslot]: e.matmul(
                            p[:], lc, rc, start=(k == 0), stop=(k == KD - 1)))
                    em.inc_last("tensor", "sPE", 1)
                    em.wait("vector", "sPE")
                    if scale is None:
                        dv(lambda e, p=pj[pslot], d=dest: e.tensor_copy(out=d, in_=p[:]))
                    else:
                        dv(lambda e, p=pj[pslot], d=dest, sc=scale: e.tensor_scalar(
                            out=d, in0=p[:], scalar1=sc, scalar2=None, op0=ALU.mult))
                    pj_cons[pslot] = em.counts["sDV"]

                for half in range(2):
                    if half == 1:
                        em.wait("gpsimd", "sPE")
                    dma_gp(wbuf3_1024,
                           wq[:, half * 1024:(half + 1) * 1024]
                           .rearrange("(k p) n -> p k n", p=128))
                    em.wait("tensor", "sGP")
                    for hh in range(8):
                        h = half * 8 + hh
                        for tt in range(2):
                            proj_group(
                                lambda k, hh=hh: wbuf[:, k * 1024 + hh * 128:
                                                      k * 1024 + hh * 128 + 128],
                                lambda k, tt=tt: xnT[:, k * TALL + 512 + tt * 512:
                                                     k * TALL + 512 + tt * 512 + 512],
                                qT[:, h * 1024 + tt * 512:h * 1024 + tt * 512 + 512],
                                (hh * 2 + tt) % 2, scale=CSC)
                em.wait("gpsimd", "sPE")
                dma_gp(wbuf3_512[:, 0:KD, :],
                       wk[:, :].rearrange("(k p) n -> p k n", p=128))
                dma_gp(wbuf3_512[:, KD:2 * KD, :],
                       wv[:, :].rearrange("(k p) n -> p k n", p=128))
                em.wait("tensor", "sGP")
                for g in range(4):
                    for tt in range(3):
                        proj_group(
                            lambda k, g=g: wbuf[:, k * 512 + g * 128:
                                                k * 512 + g * 128 + 128],
                            lambda k, tt=tt: xnT[:, k * TALL + tt * 512:
                                                 k * TALL + tt * 512 + 512],
                            kfm[:, g * TALL + tt * 512:g * TALL + tt * 512 + 512],
                            tt % 2)
                for tc in range(NCA):
                    proj_group(
                        lambda k, tc=tc: xnT[:, k * TALL + tc * 128:
                                             k * TALL + tc * 128 + 128],
                        lambda k: wbuf[:, KD * 512 + k * 512:KD * 512 + k * 512 + 512],
                        vtm[:, tc * 512:(tc + 1) * 512],
                        tc % 2)
        tap("qT", qT[:], [128, NH * TOWN], BF16)
        tap("kfm", kfm[:], [128, 4 * TALL], BF16)
        tap("vtm", vtm[:], [128, NCA * 512], BF16)
        em.phase_barrier(exclude=("sCL",))

        with ExitStack() as sc_bc:
            se4 = sb(sc_bc, "se4", [128, 8], F32)
            nm4 = sb(sc_bc, "nm4", [128, 8], F32)
            rs4 = sb(sc_bc, "rs4", [128, 8], F32)
            Pn = [sb(sc_bc, "PnA", [128, 4 * 640], BF16),
                  sb(sc_bc, "PnB", [128, 4 * 640], BF16)]
            PT = [sb(sc_bc, "PTA", [128, 8 * 512], BF16),
                  sb(sc_bc, "PTB", [128, 8 * 512], BF16)]
            af2 = [sb(sc_bc, "af2A", [128, 512], BF16),
                   sb(sc_bc, "af2B", [128, 512], BF16)]
            tcols = sb(sc_bc, "tcols", [128, NCO * 512], F32)
            sg = sb(sc_bc, "sg", [128, 512], F32)
            qwgT = [sb(sc_bc, "qwgT0b", [128, KD * 512], BF16),
                    sb(sc_bc, "qwgT1b", [128, KD * 512], BF16)]
            qwuT = [sb(sc_bc, "qwuT0b", [128, KD * 512], BF16),
                    sb(sc_bc, "qwuT1b", [128, KD * 512], BF16)]
            sgb = [sg, sb(sc_bc, "sg1", [128, 512], F32)]
            ps_sc = psm(sc_bc, "ps_sc", [128, 1024], F32)
            ps_pt = psm(sc_bc, "ps_pt", [128, 1024], BF16)
            ps_av = psm(sc_bc, "ps_av", [128, 512], F32)
            pg = [psm(sc_bc, "pg0", [128, 512], F32),
                  psm(sc_bc, "pg1", [128, 512], F32)]
            pu = [psm(sc_bc, "pu0", [128, 512], F32),
                  psm(sc_bc, "pu1", [128, 512], F32)]
            PT8 = [PT[0][:].rearrange("p (a b) -> p a b", b=512),
                   PT[1][:].rearrange("p (a b) -> p a b", b=512)]
            gp(lambda e: e.memset(PT[0][:], 0.0))
            gp(lambda e: e.memset(PT[1][:], 0.0))
            em.wait("tensor", "sGI")
            em.wait("vector", "sGP")
            em.wait("vector", "sGI")
            bst = {"sc_cons": 0, "av_cons": 0, "pt_cons": [0, 0],
                   "pt_cons_dv": [0, 0], "pn_cons": [0, 0], "se_cons": [0, 0],
                   "af_cons": [0, 0]}
            state = {"pg_cons": [0, 0], "pu_cons": [0, 0],
                     "tcols_cons": 0, "qw_cons": [0, 0], "sg_cons": [0, 0]}

            def emit_window(w, h):
                g = h // 4
                whsl = (w * NH + h) % 2
                for j in range(4):
                    em.wait("tensor", "sAC", bst["sc_cons"])
                    qcol = h * 1024 + w * 512 + j * 128
                    kcol = g * TALL + w * 512 + j * 128
                    pe(lambda e, qc=qcol, kc=kcol: e.matmul(
                        ps_sc[:, 0:512], qT[:, qc:qc + 128],
                        kfm[:, kc:kc + 512], start=True, stop=True))
                    pe(lambda e, qc=qcol, kc=kcol: e.matmul(
                        ps_sc[:, 512:640], qT[:, qc:qc + 128],
                        kfm[:, kc + 512:kc + 640], start=True, stop=True))
                    em.inc_last("tensor", "sPE", 1)
                    em.wait("vector", "sPE")
                    dv(lambda e: e.tensor_tensor(
                        out=ps_sc[:, 0:128], in0=ps_sc[:, 0:128], in1=maskp[:],
                        op=ALU.add))
                    dv(lambda e: e.tensor_tensor(
                        out=ps_sc[:, 512:640], in0=ps_sc[:, 512:640], in1=maskc[:],
                        op=ALU.add))
                    if w == 0:
                        em.bar("vector", "sDV")
                        dv(lambda e, j=j: e.tensor_tensor(
                            out=ps_sc[:, 0:512], in0=ps_sc[:, 0:512],
                            in1=m0sb[:, j * 512:(j + 1) * 512], op=ALU.add))
                    nmc = whsl * 4 + j
                    em.bar("vector", "sDV")
                    dv(lambda e, c=nmc: e.reduce_max(
                        out=nm4[:, c:c + 1], in_=ps_sc[:, 0:640], axis=AXX,
                        negate=True))
                    em.bar("vector", "sDV")
                    em.wait("scalar", "sDV")
                    em.wait("scalar", "sPE", bst["pn_cons"][whsl])
                    em.wait("scalar", "sDV", bst["se_cons"][whsl])
                    ac(lambda e, j=j, ws=whsl, c=nmc: e.activation(
                        out=Pn[ws][:, j * 640:(j + 1) * 640], in_=ps_sc[:, 0:640],
                        func=AF.Exp, scale=1.0, bias=nm4[:, c:c + 1],
                        accum_out=se4[:, c:c + 1]))
                    bst["sc_cons"] = em.counts["sAC"]
                em.wait("vector", "sAC")
                c0 = whsl * 4
                dv(lambda e, c0=c0: e.reciprocal(
                    out=rs4[:, c0:c0 + 4], in_=se4[:, c0:c0 + 4]))
                em.bar("vector", "sDV")
                bst["se_cons"][whsl] = em.counts["sDV"]
                for j in range(4):
                    dv(lambda e, j=j, ws=whsl, c0=c0: e.tensor_scalar(
                        out=Pn[ws][:, j * 640:(j + 1) * 640],
                        in0=Pn[ws][:, j * 640:(j + 1) * 640],
                        scalar1=rs4[:, c0 + j:c0 + j + 1],
                        scalar2=None, op0=ALU.mult))
                em.wait("tensor", "sDV")
                for j in range(4):
                    em.wait("tensor", "sAC", bst["pt_cons"][whsl])
                    em.wait("tensor", "sDV", bst["pt_cons_dv"][whsl])
                    for l in range(5):
                        pe(lambda e, j=j, l=l, ws=whsl: e.transpose(
                            ps_pt[:, l * 128:(l + 1) * 128],
                            Pn[ws][:, j * 640 + l * 128:j * 640 + l * 128 + 128],
                            identb[:]))
                    em.inc_last("tensor", "sPE", 1)
                    if j % 2 == 0:
                        em.wait("scalar", "sPE")
                        ac(lambda e, j=j, ws=whsl: e.copy(
                            out=PT8[ws][:, j:j + 5, j * 128:j * 128 + 128],
                            in_=ps_pt[:, 0:640].rearrange("p (l t) -> p l t", l=5)))
                        bst["pt_cons"][whsl] = em.counts["sAC"]
                    else:
                        em.wait("vector", "sPE")
                        dv(lambda e, j=j, ws=whsl: e.tensor_copy(
                            out=PT8[ws][:, j:j + 5, j * 128:j * 128 + 128],
                            in_=ps_pt[:, 0:640].rearrange("p (l t) -> p l t", l=5)))
                        bst["pt_cons_dv"][whsl] = em.counts["sDV"]
                bst["pn_cons"][whsl] = em.counts["sPE"]
                em.wait("tensor", "sAC", bst["av_cons"])
                em.wait("tensor", "sAC", bst["pt_cons"][whsl])
                em.wait("tensor", "sDV", bst["pt_cons_dv"][whsl])
                for gk in range(8):
                    tcv = w * 4 + gk
                    pe(lambda e, gk=gk, tcv=tcv, g=g, ws=whsl: e.matmul(
                        ps_av[:, 0:512],
                        vtm[:, tcv * 512 + g * 128:tcv * 512 + g * 128 + 128],
                        PT[ws][:, gk * 512:(gk + 1) * 512],
                        start=(gk == 0), stop=(gk == 7)))
                em.inc_last("tensor", "sPE", 1)
                em.wait("scalar", "sPE")
                em.wait("scalar", "sGP", bst["af_cons"][whsl])
                ac(lambda e, ws=whsl: e.copy(out=af2[ws][:], in_=ps_av[:]))
                em.inc_last("scalar", "sAC", 0) if False else None
                em.wait("gpsimd", "sAC")
                dma_gp(afm_d[:, (w * NH + h) * 512:(w * NH + h) * 512 + 512],
                       af2[whsl][:])
                bst["af_cons"][whsl] = em.counts["sGP"]
                bst["av_cons"] = em.counts["sAC"]

            def emit_c_gathers(ic):
                wsl = ic % 2
                em.wait("sync", "sPE", state["qw_cons"][wsl])
                for t in range(4):
                    icc = 4 * ic + t
                    cc = icc // 6
                    lb = (cc * 280 + (icc % 6) * 16) * 128
                    dma_sync(qwgT[wsl][:].rearrange("p (k n) -> p k n", k=KD)
                             [:, :, t * 128:(t + 1) * 128],
                             cout[lb:lb + KD * 128, :]
                             .rearrange("(k p) n -> p k n", p=128))
                    lb2 = (cc * 280 + 96 + (icc % 6) * 16) * 128
                    dma_sync(qwuT[wsl][:].rearrange("p (k n) -> p k n", k=KD)
                             [:, :, t * 128:(t + 1) * 128],
                             cout[lb2:lb2 + KD * 128, :]
                             .rearrange("(k p) n -> p k n", p=128))
                state["ld_after_gather"] = em.counts["sLD"]

            def emit_c_tc(ic, tc):
                wsl = ic % 2
                sl = tc % 2
                em.wait("tensor", "sLD", state["ld_after_gather"])
                em.wait("tensor", "sAC", state["pg_cons"][sl])
                for k in range(KD):
                    pe(lambda e, k=k, tc=tc, w=wsl, sl=sl: e.matmul(
                        pg[sl][:],
                        qxT[:, k * TOWN + tc * 128:k * TOWN + tc * 128 + 128],
                        qwgT[w][:, k * 512:(k + 1) * 512],
                        start=(k == 0), stop=(k == KD - 1)))
                em.inc_last("tensor", "sPE", 1)
                em.wait("tensor", "sDV", state["pu_cons"][sl])
                for k in range(KD):
                    pe(lambda e, k=k, tc=tc, w=wsl, sl=sl: e.matmul(
                        pu[sl][:],
                        qxT[:, k * TOWN + tc * 128:k * TOWN + tc * 128 + 128],
                        qwuT[w][:, k * 512:(k + 1) * 512],
                        start=(k == 0), stop=(k == KD - 1)))
                em.inc_last("tensor", "sPE", 1)
                em.wait("scalar", "sPE", em.counts["sPE"] - 1)
                em.wait("scalar", "sDV", state["sg_cons"][sl])
                ac(lambda e, tc=tc, sl=sl: e.activation(
                    out=sgb[sl][:], in_=pg[sl][:], func=AF.Silu,
                    scale=dxs[:, tc:tc + 1]))
                state["pg_cons"][sl] = em.counts["sAC"]
                em.wait("vector", "sPE")
                em.wait("vector", "sAC")
                em.wait("vector", "sGP", state["tcols_cons"])
                dv(lambda e, tc=tc, sl=sl: e.scalar_tensor_tensor(
                    out=tcols[:, tc * 512:(tc + 1) * 512], in0=pu[sl][:],
                    scalar=dxs[:, tc:tc + 1], in1=sgb[sl][:],
                    op0=ALU.mult, op1=ALU.mult))
                state["pu_cons"][sl] = em.counts["sDV"]
                state["sg_cons"][sl] = em.counts["sDV"]
                em.bar("vector", "sDV")
                dv(lambda e, tc=tc: e.tensor_reduce(
                    out=sm["tmp1"][:], in_=tcols[:, tc * 512:(tc + 1) * 512],
                    axis=AXX, op=ALU.max, apply_absolute_value=True))
                em.bar("vector", "sDV")
                if ic == 0:
                    dv(lambda e, tc=tc: e.tensor_copy(
                        out=rmt[:, tc:tc + 1], in_=sm["tmp1"][:]))
                else:
                    dv(lambda e, tc=tc: e.tensor_tensor(
                        out=rmt[:, tc:tc + 1], in0=rmt[:, tc:tc + 1],
                        in1=sm["tmp1"][:], op=ALU.max))
                em.bar("vector", "sDV")

            def emit_c_spill(ic):
                state["qw_cons"][ic % 2] = em.counts["sPE"]
                em.wait("gpsimd", "sDV")
                dma_gp(tsc[:, ic * 512:(ic + 1) * 512]
                       .rearrange("(tc p) c -> p tc c", p=128),
                       tcols[:].rearrange("p (tc c) -> p tc c", tc=NCO))
                state["tcols_cons"] = em.counts["sGP"]

            wins = [(w, h) for w in range(2) for h in range(NH)]
            wi = 0
            nemit = 0
            em.wait("sync", "sCL")
            for ic in range(NIC):
                emit_c_gathers(ic)
                if ic == 0:
                    tap("qwgT0", qwgT[0][:], [128, KD * 512], BF16)
                    tap("qwuT0", qwuT[0][:], [128, KD * 512], BF16)
                for tc in range(NCO):
                    emit_c_tc(ic, tc)
                    nemit += 1
                    if nemit % 3 == 0 and wi < len(wins):
                        emit_window(*wins[wi])
                        wi += 1
                if ic == 0:
                    tap("tcols0", tcols[:], [128, NCO * 512], F32)
                emit_c_spill(ic)
            while wi < len(wins):
                emit_window(*wins[wi])
                wi += 1

        tap("rmt", rmt[:], [128, NCO], F32)
        em.phase_barrier()

        qtT = sb(root, "qtT", [128, NCO * KI * 128], FP8, side="right")
        with ExitStack() as sc_c2:
                tbuf = sb(sc_c2, "tbuf", [128, II], F32)
                qt8 = sb(sc_c2, "qt8", [128, II], FP8)
                qtbf = sb(sc_c2, "qtbf", [128, II], BF16)
                c2ri = sb(sc_c2, "c2ri", [128, 2], F32)
                c2sc = sb(sc_c2, "c2sc", [128, 2], F32)
                pTc = [psm(sc_c2, "pTcA", [128, 1024], BF16),
                       psm(sc_c2, "pTcB", [128, 1024], BF16)]
                state = {"pTc_cons": [0, 0], "pTc_rot": 0}
                tbuf_cons = 0
                c2_cons = [0, 0]
                for tc in range(NCO):
                    par = tc % 2
                    em.wait("sync", "sAC", tbuf_cons)
                    dma_sync(tbuf[:], tsc[tc * 128:(tc + 1) * 128, :])
                    em.wait("vector", "sAC", c2_cons[par])
                    dv(lambda e, tc=tc, par=par: e.reciprocal(
                        out=c2ri[:, par:par + 1], in_=rmt[:, tc:tc + 1]))
                    em.bar("vector", "sDV")
                    dv(lambda e, par=par: e.tensor_scalar(
                        out=c2sc[:, par:par + 1], in0=c2ri[:, par:par + 1],
                        scalar1=224.0, scalar2=None, op0=ALU.mult))
                    dv(lambda e, tc=tc: e.tensor_scalar(
                        out=sts[:, tc:tc + 1], in0=rmt[:, tc:tc + 1],
                        scalar1=1.0 / 224.0, scalar2=None, op0=ALU.mult))
                    em.wait("scalar", "sDV")
                    em.wait("scalar", "sLD")
                    ac(lambda e, par=par: e.activation(out=qt8[:], in_=tbuf[:], func=AF.Copy,
                                                       scale=c2sc[:, par:par + 1]))
                    tbuf_cons = em.counts["sAC"]
                    c2_cons[par] = em.counts["sAC"]
                    em.wait("vector", "sAC")
                    dv(lambda e: e.tensor_copy(out=qtbf[:], in_=qt8[:]))
                    em.wait("tensor", "sDV")
                    for batch in range(6):
                        nb = 8 if batch < 5 else 4
                        rot = state["pTc_rot"] % 2
                        state["pTc_rot"] += 1
                        em.wait("tensor", "sDV", state["pTc_cons"][rot])
                        for i in range(nb):
                            kI = batch * 8 + i
                            pe(lambda e, i=i, kI=kI, r=rot: e.transpose(
                                pTc[r][:, i * 128:(i + 1) * 128],
                                qtbf[:, kI * 128:(kI + 1) * 128], identb[:]))
                        em.inc_last("tensor", "sPE", 1)
                        em.wait("vector", "sPE")
                        dv(lambda e, tc=tc, batch=batch, nb=nb, r=rot: e.tensor_copy(
                            out=qtT[:, tc * KI * 128 + batch * 1024:
                                    tc * KI * 128 + batch * 1024 + nb * 128],
                            in_=pTc[r][:, 0:nb * 128]))
                        state["pTc_cons"][rot] = em.counts["sDV"]

        tap("qtT", qtT[:], [128, NCO * KI * 128], FP8)
        tap("sts", sts[:], [128, NCO], F32)
        sc_qkv.close()
        em.phase_barrier()

        with ExitStack() as sc_d:
            afm2 = sb(sc_d, "attnfm_d", [128, 2 * NH * 512], BF16)
            qwdT = sb(sc_d, "qwdT", [128, KI * 512], BF16)
            dma_sync(afm2[:], afm_d)
            em.wait("tensor", "sLD")
            py = [psm(sc_d, "py0", [128, 512], F32),
                  psm(sc_d, "py1", [128, 512], F32)]
            pa = [psm(sc_d, "pa0", [128, 512], F32),
                  psm(sc_d, "pa1", [128, 512], F32)]
            pTd = [psm(sc_d, "pTdA", [128, 1024], BF16),
                   psm(sc_d, "pTdB", [128, 1024], BF16)]
            dstate = {"py_cons": [0, 0], "pa_cons": [0, 0], "pTd_cons": [0, 0],
                      "pTd_rot": 0, "asb_cons": [0, 0], "osb_cons": [0, 0]}

            for oc in range(4):
                em.phase_barrier()
                em.wait("sync", "sPE", dstate.get("qwd_cons", 0))
                em.wait("sync", "sCL")
                for t in range(4):
                    dmk = 4 * oc + t
                    cc = dmk // 2
                    lb = (cc * 280 + 192 + (dmk % 2) * KI) * 128
                    dma_sync(qwdT[:].rearrange("p (k n) -> p k n", k=KI)
                             [:, :, t * 128:(t + 1) * 128],
                             cout[lb:lb + KI * 128, :]
                             .rearrange("(k p) n -> p k n", p=128))
                em.wait("tensor", "sLD")
                em.phase_barrier()
                with ExitStack() as sc_dm:
                    wobuf = sb(sc_dm, "wobuf%d" % oc, [128, KD * 512], BF16)
                    xc = sb(sc_dm, "xc%d" % oc, [128, D], F32)
                    asb = sb(sc_dm, "asb%d" % oc, [128, 512], F32)
                    o1 = sb(sc_dm, "o1%d" % oc, [128, 512], F32)
                    osb = sb(sc_dm, "osb%d" % oc, [128, 512], F32)
                    em.wait("gpsimd", "sPE")
                    dma_gp(wobuf[:].rearrange("p (k n) -> p k n", k=KD),
                           wo[:, oc * 512:(oc + 1) * 512]
                           .rearrange("(k p) n -> p k n", p=128))
                    xc_cons = 0
                    em.wait("tensor", "sDV")
                    if oc == 0:
                        tap("qwdT0", qwdT[:], [128, KI * 512], BF16)
                    asb2 = [asb, sb(sc_dm, "asb%db" % oc, [128, 512], F32)]
                    o12 = [o1, sb(sc_dm, "o1%db" % oc, [128, 512], F32)]
                    osb2 = [osb, sb(sc_dm, "osb%db" % oc, [128, 512], F32)]
                    for tc in range(NCO):
                        sl = tc % 2
                        w = tc // 4
                        tl = tc % 4
                        em.wait("sync", "sDV", xc_cons)
                        dma_sync(xc[:], xh[512 + tc * 128:512 + tc * 128 + 128, :])
                        em.wait("tensor", "sDV", dstate["py_cons"][sl])
                        for kI in range(KI):
                            pe(lambda e, kI=kI, tc=tc, sl=sl: e.matmul(
                                py[sl][:],
                                qtT[:, tc * KI * 128 + kI * 128:
                                    tc * KI * 128 + kI * 128 + 128],
                                qwdT[:, kI * 512:(kI + 1) * 512],
                                start=(kI == 0), stop=(kI == KI - 1)))
                        em.inc_last("tensor", "sPE", 1)
                        em.wait("tensor", "sAC", dstate["pa_cons"][sl])
                        em.wait("tensor", "sGP")
                        for h in range(NH):
                            pe(lambda e, h=h, w=w, tl=tl, sl=sl, wob=wobuf: e.matmul(
                                pa[sl][:],
                                afm2[:, (w * NH + h) * 512 + tl * 128:
                                     (w * NH + h) * 512 + tl * 128 + 128],
                                wob[:, h * 512:(h + 1) * 512],
                                start=(h == 0), stop=(h == NH - 1)))
                        em.inc_last("tensor", "sPE", 1)
                        em.wait("scalar", "sPE")
                        em.wait("scalar", "sDV", dstate["asb_cons"][sl])
                        ac(lambda e, sl=sl, ab=asb2: e.copy(out=ab[sl][:], in_=pa[sl][:]))
                        dstate["pa_cons"][sl] = em.counts["sAC"]
                        em.wait("vector", "sPE", em.counts["sPE"] - 1)
                        em.wait("vector", "sAC")
                        dv(lambda e, tc=tc, sl=sl, ob=o12, ab=asb2: e.scalar_tensor_tensor(
                            out=ob[sl][:], in0=py[sl][:], scalar=sts[:, tc:tc + 1],
                            in1=ab[sl][:], op0=ALU.mult, op1=ALU.add))
                        dstate["py_cons"][sl] = em.counts["sDV"]
                        dstate["asb_cons"][sl] = em.counts["sDV"]
                        em.bar("vector", "sDV")
                        em.wait("vector", "sLD")
                        em.wait("vector", "sOUT", dstate["osb_cons"][sl])
                        dv(lambda e, oc=oc, sl=sl, ob=o12, osbb=osb2, xcb=xc:
                           e.tensor_tensor(
                            out=osbb[sl][:], in0=ob[sl][:],
                            in1=xcb[:, oc * 512:(oc + 1) * 512], op=ALU.add))
                        xc_cons = em.counts["sDV"]
                        em.wait("gpsimd", "sDV")
                        dma_out(yo[tc * 128:(tc + 1) * 128, oc * 512:(oc + 1) * 512],
                                osb2[sl][:])
                        dstate["osb_cons"][sl] = em.counts["sOUT"]
                    dstate["qwd_cons"] = em.counts["sPE"]

        if debug == 2:
            tap("qtT_end", qtT[:], [128, NCO * KI * 128], FP8)
            tap("sts_end", sts[:], [128, NCO], F32)
            tap("dxs_end", dxs[:], [128, NCO], F32)
            tap("qxT_end", qxT[:], [128, KD * TOWN], FP8)
        em.wait("gpsimd", "sOUT")

        with ExitStack() as sems_ctx:
            sems = {n: sems_ctx.enter_context(nc.semaphore(n)) for n in SEMS}
            with nc.Block() as block:
                em.replay(block, sems)

    return nc


_NC_CACHE = {}


def kernel(L0_attn_x, L0_attn_Wq, L0_attn_Wk, L0_attn_Wv, L0_attn_Wo,
           L0_ffn_w_gate, L0_ffn_w_up, L0_ffn_w_down,
           L0_ffn_up_reorder_index, L0_ffn_down_reorder_index):
    x = np.ascontiguousarray(np.asarray(L0_attn_x, dtype=np.float32)[0])
    Wq = np.ascontiguousarray(L0_attn_Wq, dtype=np.float32)
    Wk = np.ascontiguousarray(L0_attn_Wk, dtype=np.float32)
    Wv = np.ascontiguousarray(L0_attn_Wv, dtype=np.float32)
    Wo = np.ascontiguousarray(L0_attn_Wo, dtype=np.float32)
    wg_ = np.ascontiguousarray(L0_ffn_w_gate, dtype=np.float32)
    wu_ = np.ascontiguousarray(L0_ffn_w_up, dtype=np.float32)
    di = np.asarray(L0_ffn_down_reorder_index)
    wd2 = np.ascontiguousarray(
        np.asarray(L0_ffn_w_down, dtype=np.float32)[:, np.argsort(di)])
    wgs_c, wus_c, wds_c = [], [], []
    for c in range(8):
        gpad = np.ones((768, D), np.float32)
        upad = np.ones((768, D), np.float32)
        lo8 = c * 768
        n = max(0, min(768, II - lo8))
        if n:
            gpad[:n] = wg_[lo8:lo8 + n]
            upad[:n] = wu_[lo8:lo8 + n]
        wgs_c.append(gpad)
        wus_c.append(upad)
        wds_c.append(np.ascontiguousarray(wd2[c * 256:(c + 1) * 256]))

    T = x.shape[0]
    ncore = 8
    m0_core0 = np.zeros((4, 128, 512), np.float32)
    for j in range(4):
        m0_core0[j, :, 0:(4 - j) * 128] = NEG
    m0_zero = np.zeros((4, 128, 512), np.float32)
    in_maps = []
    for c in range(ncore):
        lo = c * TOWN
        xhc = np.zeros((TALL, D), np.float32)
        if c == 0:
            xhc[512:] = x[0:TOWN]
        else:
            xhc[:] = x[lo - 512:lo + TOWN]
        in_maps.append({
            "xh": xhc, "wq": Wq, "wk": Wk, "wv": Wv, "wo": Wo,
            "wgs": wgs_c[c], "wus": wus_c[c], "wds": wds_c[c],
            "m0": m0_core0 if c == 0 else m0_zero,
        })

    if "v1" not in _NC_CACHE:
        _NC_CACHE["v1"] = build()
    nc = _NC_CACHE["v1"]
    res = run_bass_kernel_spmd(nc, in_maps, core_ids=list(range(ncore)))
    y = np.concatenate([res.results[c]["y"] for c in range(ncore)], axis=0)
    return y.reshape(1, T, D).astype(np.float32)


# version 19
# speedup vs baseline: 1.0890x; 1.0890x over previous
import numpy as np
import concourse.bass as bass
import concourse.mybir as mybir
from concourse.bass_utils import run_bass_kernel_spmd
from contextlib import ExitStack

F32 = mybir.dt.float32
BF16 = mybir.dt.bfloat16
FP8 = mybir.dt.float8e4
AF = mybir.ActivationFunctionType
ALU = mybir.AluOpType
AXX = mybir.AxisListType.X

D = 2048
KD = 16
WIN = 512
NH = 16
HD = 128
II = 5632
KI = 44
NIC = 11
TOWN = 1024
TALL = 1536
NCO = 8
NCA = 12
CSC = float(1.0 / np.sqrt(HD))
NEG = -1e30


class Em:

    ENGS = ("sync", "gpsimd", "scalar", "tensor", "vector")

    def __init__(self):
        self.streams = {e: [] for e in self.ENGS}
        self.counts = {}
        self.watermark = {}
        self.last = {e: None for e in self.ENGS}

    def op(self, eng, fn, inc=None):
        rec = {"fn": fn, "incs": []}
        if inc is not None:
            self.add_inc(rec, inc[0], inc[1])
        self.streams[eng].append(("op", rec))
        self.last[eng] = rec
        return rec

    def add_inc(self, rec, sem, n):
        rec["incs"].append((sem, n))
        self.counts[sem] = self.counts.get(sem, 0) + n

    def inc_last(self, eng, sem, n=1):
        self.add_inc(self.last[eng], sem, n)

    def wait(self, eng, sem, cnt=None):
        if cnt is None:
            cnt = self.counts.get(sem, 0)
        if cnt <= 0:
            return
        key = (eng, sem)
        if self.watermark.get(key, -1) >= cnt:
            return
        self.watermark[key] = cnt
        self.streams[eng].append(("wait", sem, cnt))

    def bar(self, eng, sem):
        rec = self.last[eng]
        if not any(s == sem for s, _ in rec["incs"]):
            self.add_inc(rec, sem, 1)
        self.wait(eng, sem, self.counts[sem])

    def phase_barrier(self, exclude=()):
        for eng in self.ENGS:
            for sem in list(self.counts):
                if sem in exclude:
                    continue
                self.wait(eng, sem)

    def replay(self, block, sems):
        for eng in self.ENGS:
            stream = self.streams[eng]
            if not stream:
                continue

            def body(engine, stream=stream):
                for kind, *rest in stream:
                    if kind == "wait":
                        sem, cnt = rest
                        engine.wait_ge(sems[sem], cnt)
                    else:
                        rec = rest[0]
                        ins = rec["fn"](engine)
                        for sem, n in rec["incs"]:
                            ins.then_inc(sems[sem], n)
            getattr(block, eng)(body)


def build(debug=False, no_collective=False):
    nc = bass.Bass("TRN2", target_bir_lowering=False)

    xh = nc.dram_tensor("xh", [TALL, D], F32, kind="ExternalInput").ap()
    wq = nc.dram_tensor("wq", [D, D], F32, kind="ExternalInput").ap()
    wk = nc.dram_tensor("wk", [D, 512], F32, kind="ExternalInput").ap()
    wv = nc.dram_tensor("wv", [D, 512], F32, kind="ExternalInput").ap()
    wo = nc.dram_tensor("wo", [D, D], F32, kind="ExternalInput").ap()
    wgs = nc.dram_tensor("wgs", [6 * 128, D], F32, kind="ExternalInput").ap()
    wus = nc.dram_tensor("wus", [6 * 128, D], F32, kind="ExternalInput").ap()
    wds = nc.dram_tensor("wds", [2 * 128, II], F32, kind="ExternalInput").ap()
    m0 = nc.dram_tensor("m0", [4, 128, 512], F32, kind="ExternalInput").ap()
    yo = nc.dram_tensor("y", [TOWN, D], F32, kind="ExternalOutput").ap()
    tsc = nc.dram_tensor("tscr", [TOWN, II], F32, kind="Internal").ap()
    afm_d = nc.dram_tensor("afm_d", [128, 2 * NH * 512], BF16, kind="Internal").ap()
    NT_SH = 280
    cin = nc.dram_tensor("cin", [NT_SH * 128, 128], BF16, kind="Internal").ap()
    cout = nc.dram_tensor("cout", [8 * NT_SH * 128, 128], BF16, kind="Internal",
                          addr_space="Local" if no_collective else "Shared").ap()

    em = Em()
    SEMS = ["sLD", "sGP", "sGI", "sPE", "sDV", "sAC", "sOUT", "sCL"]

    def dma_sync(out, in_):
        em.wait("sync", "sLD")
        em.op("sync", lambda e, o=out, i=in_: e.dma_start(out=o, in_=i), inc=("sLD", 16))

    def dma_gp(out, in_):
        em.wait("gpsimd", "sGP")
        em.op("gpsimd", lambda e, o=out, i=in_: e.dma_start(out=o, in_=i), inc=("sGP", 16))

    def dma_out(out, in_):
        em.wait("gpsimd", "sOUT")
        em.op("gpsimd", lambda e, o=out, i=in_: e.dma_start(out=o, in_=i), inc=("sOUT", 16))

    def tap(name, tile_ap, shape, dtype):
        if not debug:
            return
        d = nc.dram_tensor("dbg_" + name, shape, dtype, kind="ExternalOutput").ap()
        for s in ("sDV", "sAC", "sPE", "sGI", "sLD"):
            em.wait("gpsimd", s)
        em.op("gpsimd", lambda e, o=d, i=tile_ap: e.dma_start(out=o, in_=i),
              inc=("sOUT", 16))

    def dv(fn, inc=True):
        em.op("vector", fn, inc=("sDV", 1) if inc else None)

    def ac(fn, inc=True):
        em.op("scalar", fn, inc=("sAC", 1) if inc else None)

    def pe(fn):
        em.op("tensor", fn)

    def gp(fn):
        em.op("gpsimd", fn, inc=("sGI", 1))

    with ExitStack() as root:
        def sb(scope, name, shape, dtype=F32, side=None):
            if side is None:
                return scope.enter_context(nc.sbuf_tensor(name, shape, dtype))
            return scope.enter_context(nc.sbuf_tensor(name, shape, dtype, side=side))

        def psm(scope, name, shape, dtype=F32):
            return scope.enter_context(nc.psum_tensor(name, shape, dtype))

        identb = sb(root, "identb", [128, 128], BF16)
        maskp = sb(root, "maskp", [128, 128], F32)
        maskc = sb(root, "maskc", [128, 128], F32)
        m0sb = sb(root, "m0sb", [128, 4 * 512], F32)
        qxT = sb(root, "qxT", [128, KD * TOWN], FP8)
        dxs = sb(root, "dxs", [128, NCO], F32)
        sts = sb(root, "sts", [128, NCO], F32)
        sm = {}
        for n in ("ssum", "ssq", "vv", "m2", "sd", "rstd", "nb",
                  "rmx", "ri", "sc8", "nm", "se", "rs", "wm", "dw", "tmp1", "eps"):
            sm[n] = sb(root, "sm_" + n, [128, 1], F32)

        gp(lambda e: e.memset(sm["eps"][:], 1e-5))
        gp(lambda e: e.memset(identb[:], 0.0))
        gp(lambda e: e.memset(maskp[:], 0.0))
        gp(lambda e: e.memset(maskc[:], 0.0))
        em.bar("gpsimd", "sGI")
        gp(lambda e: e.affine_select(out=identb[:], in_=identb[:],
                                     compare_op=ALU.not_equal, fill=1.0, base=0,
                                     pattern=[[-1, 128]], channel_multiplier=1))
        em.bar("gpsimd", "sGI")
        gp(lambda e: e.affine_select(out=maskp[:], in_=maskp[:],
                                     compare_op=ALU.is_gt, fill=NEG, base=0,
                                     pattern=[[1, 128]], channel_multiplier=-1))
        em.bar("gpsimd", "sGI")
        gp(lambda e: e.affine_select(out=maskc[:], in_=maskc[:],
                                     compare_op=ALU.is_ge, fill=NEG, base=0,
                                     pattern=[[-1, 128]], channel_multiplier=1))
        dma_gp(m0sb[:].rearrange("p (j c) -> p j c", j=4), m0[:, :, :].transpose([1, 0, 2]))

        with ExitStack() as sc_s:
            sld = [sb(sc_s, "sldA", [128, II], F32), sb(sc_s, "sldB", [128, II], F32)]
            s8 = [sb(sc_s, "s8A", [128, II], FP8), sb(sc_s, "s8B", [128, II], FP8)]
            sbf = sb(sc_s, "sbf", [128, II], BF16)
            stg_t = [sb(sc_s, "stgA", [128, KI * 128], BF16),
                     sb(sc_s, "stgB", [128, KI * 128], BF16)]
            smS = {}
            for n2 in ("wmA", "wmB", "riA", "riB", "scA", "scB", "dwA", "dwB"):
                smS[n2] = sb(sc_s, "smS_" + n2, [128, 1], F32)
            pS = [psm(sc_s, "pS0", [128, 1024], BF16), psm(sc_s, "pS1", [128, 1024], BF16)]
            st_s = {"ld": [0, 0], "pS": [0, 0], "rot": 0, "stg": [0, 0], "sbf": 0}

            def stage_chunk(idx, src_rows, width, nk, tilebase):
                par = idx % 2
                P = "AB"[par]
                em.wait("sync", "sAC", st_s["ld"][par])
                dma_sync(sld[par][:, 0:width], src_rows)
                em.wait("vector", "sLD")
                dv(lambda e, par=par, w=width: e.tensor_reduce(
                    out=smS["wm" + P][:], in_=sld[par][:, 0:w], axis=AXX,
                    op=ALU.max, apply_absolute_value=True))
                em.bar("vector", "sDV")
                dv(lambda e: e.reciprocal(out=smS["ri" + P][:], in_=smS["wm" + P][:]))
                em.bar("vector", "sDV")
                dv(lambda e: e.tensor_scalar(out=smS["sc" + P][:], in0=smS["ri" + P][:],
                                             scalar1=224.0, scalar2=None, op0=ALU.mult))
                dv(lambda e: e.tensor_scalar(out=smS["dw" + P][:], in0=smS["wm" + P][:],
                                             scalar1=1.0 / 224.0, scalar2=None,
                                             op0=ALU.mult))
                em.wait("scalar", "sDV")
                ac(lambda e, par=par, w=width: e.activation(
                    out=s8[par][:, 0:w], in_=sld[par][:, 0:w], func=AF.Copy,
                    scale=smS["sc" + P][:]))
                st_s["ld"][par] = em.counts["sAC"]
                em.wait("vector", "sAC")
                em.wait("vector", "sPE", st_s["sbf"])
                dv(lambda e, par=par, w=width: e.tensor_scalar(
                    out=sbf[:, 0:w], in0=s8[par][:, 0:w],
                    scalar1=smS["dw" + P][:], scalar2=None, op0=ALU.mult))
                em.wait("tensor", "sDV")
                em.wait("tensor", "sGI")
                em.wait("vector", "sGP", st_s["stg"][par])
                nbatch = (nk + 7) // 8
                for b in range(nbatch):
                    nb = min(8, nk - b * 8)
                    rot = st_s["rot"] % 2
                    st_s["rot"] += 1
                    em.wait("tensor", "sDV", st_s["pS"][rot])
                    for i in range(nb):
                        k = b * 8 + i
                        pe(lambda e, i=i, k=k, r=rot: e.transpose(
                            pS[r][:, i * 128:(i + 1) * 128],
                            sbf[:, k * 128:(k + 1) * 128], identb[:]))
                    em.inc_last("tensor", "sPE", 1)
                    em.wait("vector", "sPE")
                    dv(lambda e, b=b, nb=nb, r=rot, par=par: e.tensor_copy(
                        out=stg_t[par][:, b * 1024:b * 1024 + nb * 128],
                        in_=pS[r][:, 0:nb * 128]))
                    st_s["pS"][rot] = em.counts["sDV"]
                st_s["sbf"] = em.counts["sPE"]
                em.wait("gpsimd", "sDV")
                em.wait("gpsimd", "sGP")
                em.op("gpsimd", lambda e, par=par, nk=nk, tb=tilebase: e.dma_start(
                    out=cin[tb * 128:(tb + nk) * 128, :]
                    .rearrange("(k p) n -> p k n", p=128),
                    in_=stg_t[par][:, 0:nk * 128]
                    .rearrange("p (k n) -> p k n", k=nk)), inc=("sGP", 16))
                st_s["stg"][par] = em.counts["sGP"]

            idx = 0
            for i6 in range(6):
                stage_chunk(idx, wgs[i6 * 128:(i6 + 1) * 128, 0:D], D, KD, i6 * 16)
                idx += 1
            for i6 in range(6):
                stage_chunk(idx, wus[i6 * 128:(i6 + 1) * 128, 0:D], D, KD, 96 + i6 * 16)
                idx += 1
            for i2 in range(2):
                stage_chunk(idx, wds[i2 * 128:(i2 + 1) * 128, 0:II], II, KI,
                            192 + i2 * KI)
                idx += 1
            em.wait("gpsimd", "sGP")
            if no_collective:
                for cc in range(8):
                    em.wait("gpsimd", "sGP")
                    em.op("gpsimd", lambda e, cc=cc: e.dma_start(
                        out=cout[cc * NT_SH * 128:(cc + 1) * NT_SH * 128, :],
                        in_=cin[:, :]), inc=("sGP", 16))
                em.wait("gpsimd", "sGP")
                em.op("gpsimd", lambda e: e.engine_nop(), inc=("sCL", 1))
            else:
                em.op("gpsimd", lambda e: e.collective_compute(
                    "AllGather", ALU.bypass, ins=[cin], outs=[cout],
                    replica_groups=[list(range(8))]), inc=("sCL", 1))
            em.phase_barrier(exclude=("sCL",))

        rmt = sb(root, "rmt", [128, NCO], F32)
        sc_qkv = root.enter_context(ExitStack())
        qT = sb(sc_qkv, "qT", [128, NH * TOWN], BF16)
        kfm = sb(sc_qkv, "kfm", [128, 4 * TALL], BF16)
        vtm = sb(sc_qkv, "vtm", [128, NCA * 512], BF16)
        with ExitStack() as sc_ab:
            xnT = sb(sc_ab, "xnT", [128, KD * TALL], BF16)

            with ExitStack() as sc_a:
                xld = sb(sc_a, "xld", [128, 2 * D], F32)
                sqscr = sb(sc_a, "sqscr", [128, D], BF16)
                xn32 = [sb(sc_a, "xn32A", [128, D], F32),
                        sb(sc_a, "xn32B", [128, D], F32)]
                xnbf = [sb(sc_a, "xnbfA", [128, D], BF16),
                        sb(sc_a, "xnbfB", [128, D], BF16)]
                qx8c = [sb(sc_a, "qx8cA", [128, D], FP8),
                        sb(sc_a, "qx8cB", [128, D], FP8)]
                qxbf = [sb(sc_a, "qxbfA", [128, D], BF16),
                        sb(sc_a, "qxbfB", [128, D], BF16)]
                smA = {}
                for n2 in ("ssum", "ssq", "vv", "m2", "sd", "rstd", "nb", "rmx",
                           "ri", "sc8"):
                    smA[n2 + "A"] = sb(sc_a, "smA_" + n2 + "A", [128, 1], F32)
                    smA[n2 + "B"] = sb(sc_a, "smA_" + n2 + "B", [128, 1], F32)
                pT = [psm(sc_a, "pTa0", [128, 1024], BF16),
                      psm(sc_a, "pTa1", [128, 1024], BF16)]
                pT_cons = [0, 0]
                acons = {"xn32": [0, 0], "xnbf": [0, 0], "qx8c": [0, 0],
                         "qxbf": [0, 0], "xld": [0, 0], "xld_dv": [0, 0],
                         "ssq": [0, 0]}

                xnT3 = xnT[:].rearrange("p (k t) -> p k t", k=KD)
                qxT3 = qxT[:].rearrange("p (k t) -> p k t", k=KD)

                def transpose_chunk(src, dest3, col0, ncols=128):
                    em.wait("tensor", "sDV")
                    em.wait("tensor", "sGI")
                    for half in range(2):
                        pbuf = pT[half]
                        em.wait("tensor", "sDV", pT_cons[half])
                        for i in range(8):
                            k = half * 8 + i
                            pe(lambda e, pb=pbuf, i=i, k=k, s=src: e.transpose(
                                pb[:, i * 128:(i + 1) * 128],
                                s[:, k * 128:(k + 1) * 128], identb[:]))
                        em.inc_last("tensor", "sPE", 1)
                        em.wait("vector", "sPE")
                        dv(lambda e, pb=pbuf, half=half, d3=dest3, c0=col0: e.tensor_copy(
                            out=d3[:, half * 8:half * 8 + 8, c0:c0 + 128],
                            in_=pb[:].rearrange("p (k t) -> p k t", k=8)))
                        pT_cons[half] = em.counts["sDV"]

                for cn in range(NCA):
                    par = cn % 2
                    P = "AB"[par]
                    buf = xld[:, par * D:par * D + D]
                    em.wait("sync", "sAC", acons["xld"][par])
                    em.wait("sync", "sDV", acons["xld_dv"][par])
                    dma_sync(buf, xh[cn * 128:(cn + 1) * 128, :])
                    em.wait("vector", "sLD")
                    dv(lambda e, b=buf, P=P: e.reduce_sum(out=smA["ssum" + P][:], in_=b,
                                                          axis=AXX))
                    em.bar("vector", "sDV")
                    em.wait("scalar", "sLD")
                    em.wait("scalar", "sDV", acons["ssq"][par])
                    ac(lambda e, b=buf, P=P: e.activation(out=sqscr[:], in_=b,
                                                          func=AF.Square,
                                                          accum_out=smA["ssq" + P][:]))
                    dv(lambda e, P=P: e.scalar_tensor_tensor(
                        out=smA["m2" + P][:], in0=smA["ssum" + P][:],
                        scalar=1.0 / (D * float(D)), in1=smA["ssum" + P][:],
                        op0=ALU.mult, op1=ALU.mult))
                    em.bar("vector", "sDV")
                    em.wait("vector", "sAC")
                    dv(lambda e, P=P: e.scalar_tensor_tensor(
                        out=smA["vv" + P][:], in0=smA["ssq" + P][:],
                        scalar=1.0 / D, in1=smA["m2" + P][:],
                        op0=ALU.mult, op1=ALU.subtract))
                    em.bar("vector", "sDV")
                    acons["ssq"][par] = em.counts["sDV"]
                    em.wait("scalar", "sDV")
                    em.wait("scalar", "sGI")
                    ac(lambda e, P=P: e.activation(out=smA["sd" + P][:],
                                                   in_=smA["vv" + P][:], func=AF.Sqrt,
                                                   bias=sm["eps"][:]))
                    em.wait("vector", "sAC")
                    dv(lambda e, P=P: e.reciprocal(out=smA["rstd" + P][:],
                                                   in_=smA["sd" + P][:]))
                    em.bar("vector", "sDV")
                    dv(lambda e, P=P: e.scalar_tensor_tensor(
                        out=smA["nb" + P][:], in0=smA["ssum" + P][:],
                        scalar=-1.0 / D, in1=smA["rstd" + P][:],
                        op0=ALU.mult, op1=ALU.mult))
                    em.wait("scalar", "sDV")
                    em.wait("scalar", "sDV", acons["xn32"][par])
                    ac(lambda e, b=buf, P=P, par=par: e.activation(
                        out=xn32[par][:], in_=b, func=AF.Identity,
                        scale=smA["rstd" + P][:], bias=smA["nb" + P][:]))
                    acons["xld"][par] = em.counts["sAC"]
                    acons["xld_dv"][par] = em.counts["sDV"]
                    em.wait("vector", "sAC")
                    em.wait("vector", "sPE", acons["xnbf"][par])
                    dv(lambda e, par=par: e.tensor_copy(out=xnbf[par][:],
                                                        in_=xn32[par][:]))
                    transpose_chunk(xnbf[par], xnT3, cn * 128)
                    acons["xnbf"][par] = em.counts["sPE"]
                    if cn >= 4:
                        co = cn - 4
                        dv(lambda e, P=P, par=par: e.tensor_reduce(
                            out=smA["rmx" + P][:], in_=xn32[par][:], axis=AXX,
                            op=ALU.max, apply_absolute_value=True))
                        em.bar("vector", "sDV")
                        dv(lambda e, P=P: e.reciprocal(out=smA["ri" + P][:],
                                                       in_=smA["rmx" + P][:]))
                        em.bar("vector", "sDV")
                        dv(lambda e, P=P: e.tensor_scalar(
                            out=smA["sc8" + P][:], in0=smA["ri" + P][:],
                            scalar1=224.0, scalar2=None, op0=ALU.mult))
                        dv(lambda e, co=co, P=P: e.tensor_scalar(
                            out=dxs[:, co:co + 1], in0=smA["rmx" + P][:],
                            scalar1=1.0 / 224.0, scalar2=None, op0=ALU.mult))
                        em.wait("scalar", "sDV")
                        ac(lambda e, P=P, par=par: e.activation(
                            out=qx8c[par][:], in_=xn32[par][:], func=AF.Copy,
                            scale=smA["sc8" + P][:]))
                        acons["xn32"][par] = em.counts["sAC"]
                        em.wait("vector", "sAC")
                        em.wait("vector", "sPE", acons["qxbf"][par])
                        dv(lambda e, par=par: e.tensor_copy(out=qxbf[par][:],
                                                            in_=qx8c[par][:]))
                        transpose_chunk(qxbf[par], qxT3, co * 128)
                        acons["qxbf"][par] = em.counts["sPE"]
                    acons["xn32"][par] = em.counts["sDV"]

            tap("xnT", xnT[:], [128, KD * TALL], BF16)
            tap("qxT", qxT[:], [128, KD * TOWN], FP8)
            tap("dxs", dxs[:], [128, NCO], F32)
            em.phase_barrier()


            with ExitStack() as sc_bp:
                wbuf = sb(sc_bp, "wbuf", [128, KD * 1024], BF16)
                pj = [psm(sc_bp, "pj0", [128, 512], F32),
                      psm(sc_bp, "pj1", [128, 512], F32)]
                pj_cons = [0, 0]
                wbuf3_1024 = wbuf[:].rearrange("p (k n) -> p k n", k=KD)
                wbuf3_512 = wbuf[:].rearrange("p (k n) -> p k n", k=2 * KD)

                def proj_group(lhs_col_fn, rhs_col_fn, dest, pslot, scale=None):
                    em.wait("tensor", "sDV", pj_cons[pslot])
                    for k in range(KD):
                        lc, rc = lhs_col_fn(k), rhs_col_fn(k)
                        pe(lambda e, lc=lc, rc=rc, k=k, p=pj[pslot]: e.matmul(
                            p[:], lc, rc, start=(k == 0), stop=(k == KD - 1)))
                    em.inc_last("tensor", "sPE", 1)
                    em.wait("vector", "sPE")
                    if scale is None:
                        dv(lambda e, p=pj[pslot], d=dest: e.tensor_copy(out=d, in_=p[:]))
                    else:
                        dv(lambda e, p=pj[pslot], d=dest, sc=scale: e.tensor_scalar(
                            out=d, in0=p[:], scalar1=sc, scalar2=None, op0=ALU.mult))
                    pj_cons[pslot] = em.counts["sDV"]

                for half in range(2):
                    if half == 1:
                        em.wait("gpsimd", "sPE")
                    dma_gp(wbuf3_1024,
                           wq[:, half * 1024:(half + 1) * 1024]
                           .rearrange("(k p) n -> p k n", p=128))
                    em.wait("tensor", "sGP")
                    for hh in range(8):
                        h = half * 8 + hh
                        for tt in range(2):
                            proj_group(
                                lambda k, hh=hh: wbuf[:, k * 1024 + hh * 128:
                                                      k * 1024 + hh * 128 + 128],
                                lambda k, tt=tt: xnT[:, k * TALL + 512 + tt * 512:
                                                     k * TALL + 512 + tt * 512 + 512],
                                qT[:, h * 1024 + tt * 512:h * 1024 + tt * 512 + 512],
                                (hh * 2 + tt) % 2, scale=CSC)
                em.wait("gpsimd", "sPE")
                dma_gp(wbuf3_512[:, 0:KD, :],
                       wk[:, :].rearrange("(k p) n -> p k n", p=128))
                dma_gp(wbuf3_512[:, KD:2 * KD, :],
                       wv[:, :].rearrange("(k p) n -> p k n", p=128))
                em.wait("tensor", "sGP")
                for g in range(4):
                    for tt in range(3):
                        proj_group(
                            lambda k, g=g: wbuf[:, k * 512 + g * 128:
                                                k * 512 + g * 128 + 128],
                            lambda k, tt=tt: xnT[:, k * TALL + tt * 512:
                                                 k * TALL + tt * 512 + 512],
                            kfm[:, g * TALL + tt * 512:g * TALL + tt * 512 + 512],
                            tt % 2)
                for tc in range(NCA):
                    proj_group(
                        lambda k, tc=tc: xnT[:, k * TALL + tc * 128:
                                             k * TALL + tc * 128 + 128],
                        lambda k: wbuf[:, KD * 512 + k * 512:KD * 512 + k * 512 + 512],
                        vtm[:, tc * 512:(tc + 1) * 512],
                        tc % 2)
        tap("qT", qT[:], [128, NH * TOWN], BF16)
        tap("kfm", kfm[:], [128, 4 * TALL], BF16)
        tap("vtm", vtm[:], [128, NCA * 512], BF16)
        em.phase_barrier(exclude=("sCL",))

        with ExitStack() as sc_bc:
            se4 = sb(sc_bc, "se4", [128, 8], F32)
            nm4 = sb(sc_bc, "nm4", [128, 8], F32)
            rs4 = sb(sc_bc, "rs4", [128, 8], F32)
            Pn = [sb(sc_bc, "PnA", [128, 4 * 640], BF16),
                  sb(sc_bc, "PnB", [128, 4 * 640], BF16)]
            PT = [sb(sc_bc, "PTA", [128, 8 * 512], BF16),
                  sb(sc_bc, "PTB", [128, 8 * 512], BF16)]
            af2 = [sb(sc_bc, "af2A", [128, 512], BF16),
                   sb(sc_bc, "af2B", [128, 512], BF16)]
            tcols = sb(sc_bc, "tcols", [128, NCO * 512], F32)
            sg = sb(sc_bc, "sg", [128, 512], F32)
            qwgT = [sb(sc_bc, "qwgT0b", [128, KD * 512], BF16),
                    sb(sc_bc, "qwgT1b", [128, KD * 512], BF16)]
            qwuT = [sb(sc_bc, "qwuT0b", [128, KD * 512], BF16),
                    sb(sc_bc, "qwuT1b", [128, KD * 512], BF16)]
            sgb = [sg, sb(sc_bc, "sg1", [128, 512], F32)]
            ps_sc = [psm(sc_bc, "ps_sc0", [128, 1024], F32),
                     psm(sc_bc, "ps_sc1", [128, 1024], F32)]
            ps_pt = psm(sc_bc, "ps_pt", [128, 1024], BF16)
            ps_av = psm(sc_bc, "ps_av", [128, 512], F32)
            pg = [psm(sc_bc, "pg0", [128, 512], F32)] * 2
            pu = [psm(sc_bc, "pu0", [128, 512], F32)] * 2
            PT8 = [PT[0][:].rearrange("p (a b) -> p a b", b=512),
                   PT[1][:].rearrange("p (a b) -> p a b", b=512)]
            gp(lambda e: e.memset(PT[0][:], 0.0))
            gp(lambda e: e.memset(PT[1][:], 0.0))
            em.wait("tensor", "sGI")
            em.wait("vector", "sGP")
            em.wait("vector", "sGI")
            bst = {"sc_cons": [0, 0], "av_cons": 0, "pt_cons": [0, 0],
                   "pt_cons_dv": [0, 0], "pn_cons": [0, 0], "se_cons": [0, 0],
                   "af_cons": [0, 0]}
            state = {"pg_cons": [0, 0], "pu_cons": [0, 0],
                     "tcols_cons": 0, "qw_cons": [0, 0], "sg_cons": [0, 0]}

            def window_steps(w, h):
                g = h // 4
                whsl = (w * NH + h) % 2
                for j in range(4):
                    slot = j % 2
                    pscr = ps_sc[slot]
                    em.wait("tensor", "sAC", bst["sc_cons"][slot])
                    qcol = h * 1024 + w * 512 + j * 128
                    kcol = g * TALL + w * 512 + j * 128
                    pe(lambda e, p=pscr, qc=qcol, kc=kcol: e.matmul(
                        p[:, 0:512], qT[:, qc:qc + 128],
                        kfm[:, kc:kc + 512], start=True, stop=True))
                    pe(lambda e, p=pscr, qc=qcol, kc=kcol: e.matmul(
                        p[:, 512:640], qT[:, qc:qc + 128],
                        kfm[:, kc + 512:kc + 640], start=True, stop=True))
                    em.inc_last("tensor", "sPE", 1)
                    em.wait("vector", "sPE")
                    dv(lambda e, p=pscr: e.tensor_tensor(
                        out=p[:, 0:128], in0=p[:, 0:128], in1=maskp[:],
                        op=ALU.add))
                    dv(lambda e, p=pscr: e.tensor_tensor(
                        out=p[:, 512:640], in0=p[:, 512:640], in1=maskc[:],
                        op=ALU.add))
                    if w == 0:
                        em.bar("vector", "sDV")
                        dv(lambda e, p=pscr, j=j: e.tensor_tensor(
                            out=p[:, 0:512], in0=p[:, 0:512],
                            in1=m0sb[:, j * 512:(j + 1) * 512], op=ALU.add))
                    nmc = whsl * 4 + j
                    em.bar("vector", "sDV")
                    dv(lambda e, p=pscr, c=nmc: e.reduce_max(
                        out=nm4[:, c:c + 1], in_=p[:, 0:640], axis=AXX,
                        negate=True))
                    em.bar("vector", "sDV")
                    em.wait("scalar", "sDV")
                    em.wait("scalar", "sPE", bst["pn_cons"][whsl])
                    em.wait("scalar", "sDV", bst["se_cons"][whsl])
                    ac(lambda e, p=pscr, j=j, ws=whsl, c=nmc: e.activation(
                        out=Pn[ws][:, j * 640:(j + 1) * 640], in_=p[:, 0:640],
                        func=AF.Exp, scale=1.0, bias=nm4[:, c:c + 1],
                        accum_out=se4[:, c:c + 1]))
                    bst["sc_cons"][slot] = em.counts["sAC"]
                    if j % 2 == 1:
                        yield
                em.wait("vector", "sAC")
                c0 = whsl * 4
                dv(lambda e, c0=c0: e.reciprocal(
                    out=rs4[:, c0:c0 + 4], in_=se4[:, c0:c0 + 4]))
                em.bar("vector", "sDV")
                bst["se_cons"][whsl] = em.counts["sDV"]
                for j in range(4):
                    dv(lambda e, j=j, ws=whsl, c0=c0: e.tensor_scalar(
                        out=Pn[ws][:, j * 640:(j + 1) * 640],
                        in0=Pn[ws][:, j * 640:(j + 1) * 640],
                        scalar1=rs4[:, c0 + j:c0 + j + 1],
                        scalar2=None, op0=ALU.mult))
                em.wait("tensor", "sDV")
                for j in range(4):
                    em.wait("tensor", "sAC", bst["pt_cons"][whsl])
                    em.wait("tensor", "sDV", bst["pt_cons_dv"][whsl])
                    for l in range(5):
                        pe(lambda e, j=j, l=l, ws=whsl: e.transpose(
                            ps_pt[:, l * 128:(l + 1) * 128],
                            Pn[ws][:, j * 640 + l * 128:j * 640 + l * 128 + 128],
                            identb[:]))
                    em.inc_last("tensor", "sPE", 1)
                    if j % 2 == 0:
                        em.wait("scalar", "sPE")
                        ac(lambda e, j=j, ws=whsl: e.copy(
                            out=PT8[ws][:, j:j + 5, j * 128:j * 128 + 128],
                            in_=ps_pt[:, 0:640].rearrange("p (l t) -> p l t", l=5)))
                        bst["pt_cons"][whsl] = em.counts["sAC"]
                    else:
                        em.wait("vector", "sPE")
                        dv(lambda e, j=j, ws=whsl: e.tensor_copy(
                            out=PT8[ws][:, j:j + 5, j * 128:j * 128 + 128],
                            in_=ps_pt[:, 0:640].rearrange("p (l t) -> p l t", l=5)))
                        bst["pt_cons_dv"][whsl] = em.counts["sDV"]
                bst["pn_cons"][whsl] = em.counts["sPE"]
                yield
                em.wait("tensor", "sAC", bst["av_cons"])
                em.wait("tensor", "sAC", bst["pt_cons"][whsl])
                em.wait("tensor", "sDV", bst["pt_cons_dv"][whsl])
                for gk in range(8):
                    tcv = w * 4 + gk
                    pe(lambda e, gk=gk, tcv=tcv, g=g, ws=whsl: e.matmul(
                        ps_av[:, 0:512],
                        vtm[:, tcv * 512 + g * 128:tcv * 512 + g * 128 + 128],
                        PT[ws][:, gk * 512:(gk + 1) * 512],
                        start=(gk == 0), stop=(gk == 7)))
                em.inc_last("tensor", "sPE", 1)
                em.wait("scalar", "sPE")
                em.wait("scalar", "sGP", bst["af_cons"][whsl])
                ac(lambda e, ws=whsl: e.copy(out=af2[ws][:], in_=ps_av[:]))
                em.inc_last("scalar", "sAC", 0) if False else None
                em.wait("gpsimd", "sAC")
                dma_gp(afm_d[:, (w * NH + h) * 512:(w * NH + h) * 512 + 512],
                       af2[whsl][:])
                bst["af_cons"][whsl] = em.counts["sGP"]
                bst["av_cons"] = em.counts["sAC"]

            def emit_c_gathers(ic):
                wsl = ic % 2
                em.wait("sync", "sPE", state["qw_cons"][wsl])
                for t in range(4):
                    icc = 4 * ic + t
                    cc = icc // 6
                    lb = (cc * 280 + (icc % 6) * 16) * 128
                    dma_sync(qwgT[wsl][:].rearrange("p (k n) -> p k n", k=KD)
                             [:, :, t * 128:(t + 1) * 128],
                             cout[lb:lb + KD * 128, :]
                             .rearrange("(k p) n -> p k n", p=128))
                    lb2 = (cc * 280 + 96 + (icc % 6) * 16) * 128
                    dma_sync(qwuT[wsl][:].rearrange("p (k n) -> p k n", k=KD)
                             [:, :, t * 128:(t + 1) * 128],
                             cout[lb2:lb2 + KD * 128, :]
                             .rearrange("(k p) n -> p k n", p=128))
                state["ld_after_gather"] = em.counts["sLD"]

            def emit_c_tc(ic, tc):
                wsl = ic % 2
                sl = tc % 2
                em.wait("tensor", "sLD", state["ld_after_gather"])
                em.wait("tensor", "sAC", state["pg_cons"][0])
                for k in range(KD):
                    pe(lambda e, k=k, tc=tc, w=wsl, sl=sl: e.matmul(
                        pg[sl][:],
                        qxT[:, k * TOWN + tc * 128:k * TOWN + tc * 128 + 128],
                        qwgT[w][:, k * 512:(k + 1) * 512],
                        start=(k == 0), stop=(k == KD - 1)))
                em.inc_last("tensor", "sPE", 1)
                em.wait("tensor", "sDV", state["pu_cons"][0])
                for k in range(KD):
                    pe(lambda e, k=k, tc=tc, w=wsl, sl=sl: e.matmul(
                        pu[sl][:],
                        qxT[:, k * TOWN + tc * 128:k * TOWN + tc * 128 + 128],
                        qwuT[w][:, k * 512:(k + 1) * 512],
                        start=(k == 0), stop=(k == KD - 1)))
                em.inc_last("tensor", "sPE", 1)
                em.wait("scalar", "sPE", em.counts["sPE"] - 1)
                em.wait("scalar", "sDV", state["sg_cons"][sl])
                ac(lambda e, tc=tc, sl=sl: e.activation(
                    out=sgb[sl][:], in_=pg[sl][:], func=AF.Silu,
                    scale=dxs[:, tc:tc + 1]))
                state["pg_cons"][0] = em.counts["sAC"]
                em.wait("vector", "sPE")
                em.wait("vector", "sAC")
                em.wait("vector", "sGP", state["tcols_cons"])
                dv(lambda e, tc=tc, sl=sl: e.scalar_tensor_tensor(
                    out=tcols[:, tc * 512:(tc + 1) * 512], in0=pu[sl][:],
                    scalar=dxs[:, tc:tc + 1], in1=sgb[sl][:],
                    op0=ALU.mult, op1=ALU.mult))
                state["pu_cons"][0] = em.counts["sDV"]
                state["sg_cons"][sl] = em.counts["sDV"]
                em.bar("vector", "sDV")
                dv(lambda e, tc=tc: e.tensor_reduce(
                    out=sm["tmp1"][:], in_=tcols[:, tc * 512:(tc + 1) * 512],
                    axis=AXX, op=ALU.max, apply_absolute_value=True))
                em.bar("vector", "sDV")
                if ic == 0:
                    dv(lambda e, tc=tc: e.tensor_copy(
                        out=rmt[:, tc:tc + 1], in_=sm["tmp1"][:]))
                else:
                    dv(lambda e, tc=tc: e.tensor_tensor(
                        out=rmt[:, tc:tc + 1], in0=rmt[:, tc:tc + 1],
                        in1=sm["tmp1"][:], op=ALU.max))
                em.bar("vector", "sDV")

            def emit_c_spill(ic):
                state["qw_cons"][ic % 2] = em.counts["sPE"]
                em.wait("gpsimd", "sDV")
                dma_gp(tsc[:, ic * 512:(ic + 1) * 512]
                       .rearrange("(tc p) c -> p tc c", p=128),
                       tcols[:].rearrange("p (tc c) -> p tc c", tc=NCO))
                state["tcols_cons"] = em.counts["sGP"]

            wins = [(w, h) for w in range(2) for h in range(NH)]
            gens = [window_steps(w, h) for (w, h) in wins]
            gi = 0

            def pump(n):
                nonlocal gi
                while n > 0 and gi < len(gens):
                    try:
                        next(gens[gi])
                        n -= 1
                    except StopIteration:
                        gi += 1
                return

            em.wait("sync", "sCL")
            for ic in range(NIC):
                emit_c_gathers(ic)
                if ic == 0:
                    tap("qwgT0", qwgT[0][:], [128, KD * 512], BF16)
                    tap("qwuT0", qwuT[0][:], [128, KD * 512], BF16)
                for tc in range(NCO):
                    emit_c_tc(ic, tc)
                    pump(1)
                    if (ic * NCO + tc) % 9 == 8:
                        pump(1)
                if ic == 0:
                    tap("tcols0", tcols[:], [128, NCO * 512], F32)
                emit_c_spill(ic)
            pump(10 ** 9)

        tap("rmt", rmt[:], [128, NCO], F32)
        em.phase_barrier()

        qtT = sb(root, "qtT", [128, NCO * KI * 128], FP8, side="right")
        with ExitStack() as sc_c2:
                tbuf = sb(sc_c2, "tbuf", [128, II], F32)
                qt8 = sb(sc_c2, "qt8", [128, II], FP8)
                qtbf = sb(sc_c2, "qtbf", [128, II], BF16)
                c2ri = sb(sc_c2, "c2ri", [128, 2], F32)
                c2sc = sb(sc_c2, "c2sc", [128, 2], F32)
                pTc = [psm(sc_c2, "pTcA", [128, 1024], BF16),
                       psm(sc_c2, "pTcB", [128, 1024], BF16)]
                state = {"pTc_cons": [0, 0], "pTc_rot": 0}
                tbuf_cons = 0
                c2_cons = [0, 0]
                for tc in range(NCO):
                    par = tc % 2
                    em.wait("sync", "sAC", tbuf_cons)
                    dma_sync(tbuf[:], tsc[tc * 128:(tc + 1) * 128, :])
                    em.wait("vector", "sAC", c2_cons[par])
                    dv(lambda e, tc=tc, par=par: e.reciprocal(
                        out=c2ri[:, par:par + 1], in_=rmt[:, tc:tc + 1]))
                    em.bar("vector", "sDV")
                    dv(lambda e, par=par: e.tensor_scalar(
                        out=c2sc[:, par:par + 1], in0=c2ri[:, par:par + 1],
                        scalar1=224.0, scalar2=None, op0=ALU.mult))
                    dv(lambda e, tc=tc: e.tensor_scalar(
                        out=sts[:, tc:tc + 1], in0=rmt[:, tc:tc + 1],
                        scalar1=1.0 / 224.0, scalar2=None, op0=ALU.mult))
                    em.wait("scalar", "sDV")
                    em.wait("scalar", "sLD")
                    ac(lambda e, par=par: e.activation(out=qt8[:], in_=tbuf[:], func=AF.Copy,
                                                       scale=c2sc[:, par:par + 1]))
                    tbuf_cons = em.counts["sAC"]
                    c2_cons[par] = em.counts["sAC"]
                    em.wait("vector", "sAC")
                    dv(lambda e: e.tensor_copy(out=qtbf[:], in_=qt8[:]))
                    em.wait("tensor", "sDV")
                    for batch in range(6):
                        nb = 8 if batch < 5 else 4
                        rot = state["pTc_rot"] % 2
                        state["pTc_rot"] += 1
                        em.wait("tensor", "sDV", state["pTc_cons"][rot])
                        for i in range(nb):
                            kI = batch * 8 + i
                            pe(lambda e, i=i, kI=kI, r=rot: e.transpose(
                                pTc[r][:, i * 128:(i + 1) * 128],
                                qtbf[:, kI * 128:(kI + 1) * 128], identb[:]))
                        em.inc_last("tensor", "sPE", 1)
                        em.wait("vector", "sPE")
                        dv(lambda e, tc=tc, batch=batch, nb=nb, r=rot: e.tensor_copy(
                            out=qtT[:, tc * KI * 128 + batch * 1024:
                                    tc * KI * 128 + batch * 1024 + nb * 128],
                            in_=pTc[r][:, 0:nb * 128]))
                        state["pTc_cons"][rot] = em.counts["sDV"]

        tap("qtT", qtT[:], [128, NCO * KI * 128], FP8)
        tap("sts", sts[:], [128, NCO], F32)
        sc_qkv.close()
        em.phase_barrier()

        with ExitStack() as sc_d:
            afm2 = sb(sc_d, "attnfm_d", [128, 2 * NH * 512], BF16)
            qwdT = sb(sc_d, "qwdT", [128, KI * 512], BF16)
            dma_sync(afm2[:], afm_d)
            em.wait("tensor", "sLD")
            py = [psm(sc_d, "py0", [128, 512], F32),
                  psm(sc_d, "py1", [128, 512], F32)]
            pa = [psm(sc_d, "pa0", [128, 512], F32),
                  psm(sc_d, "pa1", [128, 512], F32)]
            pTd = [psm(sc_d, "pTdA", [128, 1024], BF16),
                   psm(sc_d, "pTdB", [128, 1024], BF16)]
            dstate = {"py_cons": [0, 0], "pa_cons": [0, 0], "pTd_cons": [0, 0],
                      "pTd_rot": 0, "asb_cons": [0, 0], "osb_cons": [0, 0]}

            for oc in range(4):
                em.phase_barrier()
                em.wait("sync", "sPE", dstate.get("qwd_cons", 0))
                em.wait("sync", "sCL")
                for t in range(4):
                    dmk = 4 * oc + t
                    cc = dmk // 2
                    lb = (cc * 280 + 192 + (dmk % 2) * KI) * 128
                    dma_sync(qwdT[:].rearrange("p (k n) -> p k n", k=KI)
                             [:, :, t * 128:(t + 1) * 128],
                             cout[lb:lb + KI * 128, :]
                             .rearrange("(k p) n -> p k n", p=128))
                em.wait("tensor", "sLD")
                em.phase_barrier()
                with ExitStack() as sc_dm:
                    wobuf = sb(sc_dm, "wobuf%d" % oc, [128, KD * 512], BF16)
                    xc = sb(sc_dm, "xc%d" % oc, [128, D], F32)
                    asb = sb(sc_dm, "asb%d" % oc, [128, 512], F32)
                    o1 = sb(sc_dm, "o1%d" % oc, [128, 512], F32)
                    osb = sb(sc_dm, "osb%d" % oc, [128, 512], F32)
                    em.wait("gpsimd", "sPE")
                    dma_gp(wobuf[:].rearrange("p (k n) -> p k n", k=KD),
                           wo[:, oc * 512:(oc + 1) * 512]
                           .rearrange("(k p) n -> p k n", p=128))
                    xc_cons = 0
                    em.wait("tensor", "sDV")
                    if oc == 0:
                        tap("qwdT0", qwdT[:], [128, KI * 512], BF16)
                    asb2 = [asb, sb(sc_dm, "asb%db" % oc, [128, 512], F32)]
                    o12 = [o1, sb(sc_dm, "o1%db" % oc, [128, 512], F32)]
                    osb2 = [osb, sb(sc_dm, "osb%db" % oc, [128, 512], F32)]
                    for tc in range(NCO):
                        sl = tc % 2
                        w = tc // 4
                        tl = tc % 4
                        em.wait("sync", "sDV", xc_cons)
                        dma_sync(xc[:], xh[512 + tc * 128:512 + tc * 128 + 128, :])
                        em.wait("tensor", "sDV", dstate["py_cons"][sl])
                        for kI in range(KI):
                            pe(lambda e, kI=kI, tc=tc, sl=sl: e.matmul(
                                py[sl][:],
                                qtT[:, tc * KI * 128 + kI * 128:
                                    tc * KI * 128 + kI * 128 + 128],
                                qwdT[:, kI * 512:(kI + 1) * 512],
                                start=(kI == 0), stop=(kI == KI - 1)))
                        em.inc_last("tensor", "sPE", 1)
                        em.wait("tensor", "sAC", dstate["pa_cons"][sl])
                        em.wait("tensor", "sGP")
                        for h in range(NH):
                            pe(lambda e, h=h, w=w, tl=tl, sl=sl, wob=wobuf: e.matmul(
                                pa[sl][:],
                                afm2[:, (w * NH + h) * 512 + tl * 128:
                                     (w * NH + h) * 512 + tl * 128 + 128],
                                wob[:, h * 512:(h + 1) * 512],
                                start=(h == 0), stop=(h == NH - 1)))
                        em.inc_last("tensor", "sPE", 1)
                        em.wait("scalar", "sPE")
                        em.wait("scalar", "sDV", dstate["asb_cons"][sl])
                        ac(lambda e, sl=sl, ab=asb2: e.copy(out=ab[sl][:], in_=pa[sl][:]))
                        dstate["pa_cons"][sl] = em.counts["sAC"]
                        em.wait("vector", "sPE", em.counts["sPE"] - 1)
                        em.wait("vector", "sAC")
                        dv(lambda e, tc=tc, sl=sl, ob=o12, ab=asb2: e.scalar_tensor_tensor(
                            out=ob[sl][:], in0=py[sl][:], scalar=sts[:, tc:tc + 1],
                            in1=ab[sl][:], op0=ALU.mult, op1=ALU.add))
                        dstate["py_cons"][sl] = em.counts["sDV"]
                        dstate["asb_cons"][sl] = em.counts["sDV"]
                        em.bar("vector", "sDV")
                        em.wait("vector", "sLD")
                        em.wait("vector", "sOUT", dstate["osb_cons"][sl])
                        dv(lambda e, oc=oc, sl=sl, ob=o12, osbb=osb2, xcb=xc:
                           e.tensor_tensor(
                            out=osbb[sl][:], in0=ob[sl][:],
                            in1=xcb[:, oc * 512:(oc + 1) * 512], op=ALU.add))
                        xc_cons = em.counts["sDV"]
                        em.wait("gpsimd", "sDV")
                        dma_out(yo[tc * 128:(tc + 1) * 128, oc * 512:(oc + 1) * 512],
                                osb2[sl][:])
                        dstate["osb_cons"][sl] = em.counts["sOUT"]
                    dstate["qwd_cons"] = em.counts["sPE"]

        if debug == 2:
            tap("qtT_end", qtT[:], [128, NCO * KI * 128], FP8)
            tap("sts_end", sts[:], [128, NCO], F32)
            tap("dxs_end", dxs[:], [128, NCO], F32)
            tap("qxT_end", qxT[:], [128, KD * TOWN], FP8)
        em.wait("gpsimd", "sOUT")

        with ExitStack() as sems_ctx:
            sems = {n: sems_ctx.enter_context(nc.semaphore(n)) for n in SEMS}
            with nc.Block() as block:
                em.replay(block, sems)

    return nc


_NC_CACHE = {}


def kernel(L0_attn_x, L0_attn_Wq, L0_attn_Wk, L0_attn_Wv, L0_attn_Wo,
           L0_ffn_w_gate, L0_ffn_w_up, L0_ffn_w_down,
           L0_ffn_up_reorder_index, L0_ffn_down_reorder_index):
    x = np.ascontiguousarray(np.asarray(L0_attn_x, dtype=np.float32)[0])
    Wq = np.ascontiguousarray(L0_attn_Wq, dtype=np.float32)
    Wk = np.ascontiguousarray(L0_attn_Wk, dtype=np.float32)
    Wv = np.ascontiguousarray(L0_attn_Wv, dtype=np.float32)
    Wo = np.ascontiguousarray(L0_attn_Wo, dtype=np.float32)
    wg_ = np.ascontiguousarray(L0_ffn_w_gate, dtype=np.float32)
    wu_ = np.ascontiguousarray(L0_ffn_w_up, dtype=np.float32)
    di = np.asarray(L0_ffn_down_reorder_index)
    wd2 = np.ascontiguousarray(
        np.asarray(L0_ffn_w_down, dtype=np.float32)[:, np.argsort(di)])
    wgs_c, wus_c, wds_c = [], [], []
    for c in range(8):
        gpad = np.ones((768, D), np.float32)
        upad = np.ones((768, D), np.float32)
        lo8 = c * 768
        n = max(0, min(768, II - lo8))
        if n:
            gpad[:n] = wg_[lo8:lo8 + n]
            upad[:n] = wu_[lo8:lo8 + n]
        wgs_c.append(gpad)
        wus_c.append(upad)
        wds_c.append(np.ascontiguousarray(wd2[c * 256:(c + 1) * 256]))

    T = x.shape[0]
    ncore = 8
    m0_core0 = np.zeros((4, 128, 512), np.float32)
    for j in range(4):
        m0_core0[j, :, 0:(4 - j) * 128] = NEG
    m0_zero = np.zeros((4, 128, 512), np.float32)
    in_maps = []
    for c in range(ncore):
        lo = c * TOWN
        xhc = np.zeros((TALL, D), np.float32)
        if c == 0:
            xhc[512:] = x[0:TOWN]
        else:
            xhc[:] = x[lo - 512:lo + TOWN]
        in_maps.append({
            "xh": xhc, "wq": Wq, "wk": Wk, "wv": Wv, "wo": Wo,
            "wgs": wgs_c[c], "wus": wus_c[c], "wds": wds_c[c],
            "m0": m0_core0 if c == 0 else m0_zero,
        })

    if "v1" not in _NC_CACHE:
        _NC_CACHE["v1"] = build()
    nc = _NC_CACHE["v1"]
    res = run_bass_kernel_spmd(nc, in_maps, core_ids=list(range(ncore)))
    y = np.concatenate([res.results[c]["y"] for c in range(ncore)], axis=0)
    return y.reshape(1, T, D).astype(np.float32)
